# version 19 (final)
# baseline (speedup 1.0000x reference)
import numpy as np
import concourse.bass as bass
import concourse.mybir as mybir
from concourse.bass_utils import run_bass_kernel_spmd

F32 = mybir.dt.float32
BF16 = mybir.dt.bfloat16
AF = mybir.ActivationFunctionType
ALU = mybir.AluOpType
AX = mybir.AxisListType

D = 1024
ET = 8
T = 2048
S = 4096
W = 512
NW = T // W
HEADS = 8
DH = 64
HID = 2730
HTILES = 22
QT = [6, 6, 5, 5]
QOF = [0, 6, 12, 17]
NEG = -1e30
SC = T + 1
MC = 128
NMC = T // MC
ENGS = ("sync", "gpsimd", "tensor", "vector", "scalar")

NBF = 57344
A_SC = 0
M_WD = 16384
M_WV = 24576
M_WG = 32768
M_XBF = 40960
M_ZBF = 49152
M_Z2 = 53248
WQKV = 16384
KU = 0
WQKV = 16384
VT = 28672
TABQ = 36864
XBFW = 40960
QRAW = 45064
KRAW = 47112
TABK = 49160
KHAT = 51208
KTMP = 51720
Q2 = 52232
NQB = 54280
WO = 16384
RCV = 40960
PRB = 47112
PRB3 = 56584
ATT = 48392
ATT2 = 20480
OTW = 52488
QTB = 54536
SL_A = 0
SL_B = 18432
LNH = 36864
HBU = 53248
F_ZBF = 18432
F_Z2 = 22528
F_SKB = 26624
GEL = 56320
FIN_ZBF = 0
FIN_Z2 = 4096
FIN_SKB = 8192

RECV_K = 0
RECV_V = ET * W
RECV_B = ET * W + 4 * W

NF32 = 6144
PA = [0, 512]
PT1 = [1024, 1536]
PAF = [2048, 2560]
ZRO = 3072
HSND = 3584
ASTATE = 3592
SK = 4096
FINX = 0

CF_COLS = 1024
CF_BG = 0
CF_BV = 8
CF_BD = 16
CF_B2 = 24
CF_LNG = 40
CF_B1A = 72
CF_QS = 160
CF_FLAG = 168
CF_MO = 176
CF_MP = 304
CF_MW0 = 432
CB_COLS = 1600
CB_ID = 0
CB_IND = 128
CB_SEL = 160
CB_ONE = 672
CB_ONR = 673
CB_M2 = 802
CB_MW0 = 1058


class Ctx:
    def __init__(self, K, eng, e):
        self.K = K
        self.eng = eng
        self.e = e
        self.ctr = dict(K.base_ctr)
        self.waited = {g: dict(K.base_waited[g]) for g in ENGS}

    def inc(self, s, n=1):
        self.ctr[s] = self.ctr.get(s, 0) + n
        return self.ctr[s]

    def val(self, s):
        return self.ctr.get(s, 0)

    def w(self, eng, s, v):
        if v is None or v <= 0:
            return
        if self.waited[eng].get(s, 0) >= v:
            return
        self.waited[eng][s] = v
        if self.eng == eng:
            self.e.wait_ge(self.K.sems[s], v)

    def _emit(self, eng, inc, n, build):
        if eng in ("vector", "scalar"):
            s = "sV" if eng == "vector" else "sS"
            prev = self.val(s)
            v = self.inc(s)
            need_wait = prev > 0 and self.waited[eng].get(s, 0) < prev
            if self.eng == eng:
                i = build(self.e).then_inc(self.K.sems[s], 1)
                if need_wait:
                    i._wait_ge(self.K.sems[s], prev)
            if prev > 0:
                self.waited[eng][s] = max(self.waited[eng].get(s, 0), prev)
            return v
        v = self.inc(inc, n) if inc else None
        if self.eng == eng:
            i = build(self.e)
            if inc:
                i.then_inc(self.K.sems[inc], n)
        return v

    def mm(self, out, lhsT, rhs, start, stop, inc=None):
        return self._emit("tensor", inc, 1,
                          lambda e: e.matmul(out, lhsT, rhs, start=start, stop=stop))

    def trans(self, out, in_, iden, inc=None):
        return self._emit("tensor", inc, 1,
                          lambda e: e.transpose(out, in_, iden))

    def act(self, out, in_, func, bias=0.0, scale=1.0, accum=None, inc=None):
        return self._emit("scalar", inc, 1,
                          lambda e: e.activation(out, in_, func, bias=bias,
                                                 scale=scale, accum_out=accum))

    def scopy(self, out, in_, inc=None):
        return self.act(out, in_, AF.Copy, inc=inc)

    def v(self, name, *a, inc=None, **kw):
        return self._emit("vector", inc, 1,
                          lambda e: getattr(e, name)(*a, **kw))

    def dma(self, q, dst, src, inc=None, **kw):
        sem = inc or ("sWD" if q == "gpsimd" else "sXD")
        return self._emit(q, sem, 16,
                          lambda e: e.dma_start(out=dst, in_=src, **kw))

    def cc(self, kind, ins, outs, groups, inc="sCC"):
        return self._emit("gpsimd", inc, 1,
                          lambda e: e.collective_compute(
                              kind, ALU.bypass, replica_groups=groups,
                              ins=ins, outs=outs))

    def wdma(self, eng, s):
        self.w(eng, s, self.val(s))

    def nop_inc(self, eng, s, n=1):
        return self._emit(eng, s, n, lambda e: e.engine_nop()
                          if eng in ("vector", "gpsimd") else e.nop())


class Builder:
    def __init__(self, nc, sems, shared):
        self.nc = nc
        self.sems = sems
        self.shared = shared
        self.base_ctr = {}
        self.base_waited = {g: {} for g in ENGS}

    def block(self, name, fn):
        final = {}
        st0 = dict(self.shared)
        with self.nc.Block() as blk:
            for eng in ENGS:
                def section(e, eng=eng):
                    self.shared.clear()
                    self.shared.update(st0)
                    c = Ctx(self, eng, e)
                    fn(c)
                    if "ctr" in final:
                        assert final["ctr"] == c.ctr, (
                            f"replay divergence in block {name}: "
                            f"{final['ctr']} vs {c.ctr}")
                    final["ctr"] = c.ctr
                    final["waited"] = c.waited
                getattr(blk, eng)(section)
        self.base_ctr = final["ctr"]
        self.base_waited = final["waited"]
        for g in ENGS:
            for s in ("sT", "sV", "sS"):
                self.base_waited[g][s] = max(self.base_waited[g].get(s, 0),
                                             self.base_ctr.get(s, 0))


def build_program(taps=(), depth_layers=2, stop_after=None, p2_ln=4, p2_spill=True, p2_chunks=4):
    nc = bass.Bass("TRN2", target_bir_lowering=False)
    io = {}
    io["xT"] = nc.dram_tensor("xT", [128, ET * T], F32, kind="ExternalInput")
    io["outT"] = nc.dram_tensor("outT", [128, ET * T], F32, kind="ExternalOutput")
    for nm in ("wgT", "wvT", "wdT"):
        io[nm] = nc.dram_tensor(nm, [128, ET * 1024], F32, kind="ExternalInput")
    for l in range(2):
        io[f"wqkv{l}"] = nc.dram_tensor(f"wqkv{l}", [128, ET * 1536], F32, kind="ExternalInput")
        io[f"wo{l}"] = nc.dram_tensor(f"wo{l}", [128, 4 * 1024], F32, kind="ExternalInput")
        for q in range(4):
            io[f"w1q{l}_{q}"] = nc.dram_tensor(
                f"w1q{l}_{q}", [128, 2 * QT[q] * ET * 128], F32, kind="ExternalInput")
            io[f"w2q{l}_{q}"] = nc.dram_tensor(
                f"w2q{l}_{q}", [128, ET * QT[q] * 128], F32, kind="ExternalInput")
    io["tabs"] = nc.dram_tensor("tabs", [64, 4 * T], F32, kind="ExternalInput")
    io["constf"] = nc.dram_tensor("constf", [128, CF_COLS], F32, kind="ExternalInput")
    io["constb"] = nc.dram_tensor("constb", [128, CB_COLS], F32, kind="ExternalInput")
    io["x1spill"] = nc.dram_tensor("x1spill", [128, ET * T], F32)
    io["cc_h_in"] = nc.dram_tensor("cc_h_in", [128, ET], F32)
    io["cc_h_out"] = nc.dram_tensor("cc_h_out", [256, ET], F32)
    KVC = 4 * W + 4 * W + ET
    for l in range(2):
        io[f"cc_kv{l}_in"] = nc.dram_tensor(f"cc_kv{l}_in", [128, KVC], BF16)
        io[f"cc_kv{l}_out"] = nc.dram_tensor(f"cc_kv{l}_out", [256, KVC], BF16)
        io[f"cc_b{l}_in"] = nc.dram_tensor(f"cc_b{l}_in", [128, ET], BF16)
        io[f"cc_b{l}_out"] = nc.dram_tensor(f"cc_b{l}_out", [256, ET], BF16)
        io[f"cc_bq{l}_in"] = nc.dram_tensor(f"cc_bq{l}_in", [128, ET], BF16)
        io[f"cc_bq{l}_out"] = nc.dram_tensor(f"cc_bq{l}_out", [256, ET], BF16)
    tap_t = {t: nc.dram_tensor("tap_" + t, [128, ET * T], F32,
                               kind="ExternalOutput") for t in taps}

    from contextlib import ExitStack
    es = ExitStack()
    sbuf = lambda n, s, d: es.enter_context(nc.sbuf_tensor(n, s, d))
    psum = lambda n, s, d: es.enter_context(nc.psum_tensor(n, s, d))

    stream = sbuf("stream", [128, ET * SC], F32)
    AB = sbuf("AB", [128, NBF], BF16)
    AFS = sbuf("AFS", [128, NF32], F32)
    cf = sbuf("cf", [128, CF_COLS], F32)
    cb = sbuf("cb", [128, CB_COLS], BF16)
    recvh = sbuf("recvh", [128, ET], F32)
    hincol = sbuf("hincol", [128, ET], F32)
    bnd = sbuf("bnd", [128, ET], BF16)
    bndr = sbuf("bndr", [128, ET], BF16)
    PS = [psum(f"ps{i}", [128, 1024], F32) for i in range(3)]
    PT = psum("pt", [128, 2048], BF16)

    sems = {}
    for s in ("sWD", "sWA", "sWB", "sXB0", "sXB1", "sXA0", "sXA1", "sCB", "sXD", "sCC", "sT", "sV", "sS"):
        sems[s] = es.enter_context(nc.semaphore(s))
    st = {}
    K = Builder(nc, sems, st)

    def ab(off, cols, r0=0, rows=128):
        return AB[r0:r0 + rows, off:off + cols]

    def af(off, cols, r0=0, rows=128):
        return AFS[r0:r0 + rows, off:off + cols]

    def stc(et, c0, n):
        return stream[:, et * SC + c0: et * SC + c0 + n]

    def pbank(i, rows=128, cols=512, r0=0, c0=0):
        t = PS[i // 2]
        base = (i % 2) * 512
        return t[r0:r0 + rows, base + c0:base + c0 + cols]

    xr = io["xT"].ap().rearrange("p (e t) -> p e t", e=ET)

    def p0(c):
        st["cf"] = c.dma("sync", cf[:], io["constf"][:])
        st["cb"] = c.dma("gpsimd", cb[:], io["constb"][:])
        st["wd"] = c.dma("gpsimd", ab(M_WD, 8192), io["wdT"][:])
        st["wv"] = c.dma("gpsimd", ab(M_WV, 8192), io["wvT"][:])
        st["wg"] = c.dma("gpsimd", ab(M_WG, 8192), io["wgT"][:])
        c.wdma("vector", "sXD")
        for et in range(ET):
            c.v("memset", stc(et, 0, 1), 0.0)
        c.v("memset", af(ZRO, 512), 0.0, inc="sV")
        st["p0v"] = c.val("sV")

    CH = 512
    NCH = T // CH

    def p1(c):
        xbf_guard = [None, None]
        bfree = [None] * 4
        for ch in range(NCH):
            slot = ch % 2
            xo = M_XBF + slot * 4096
            if xbf_guard[slot]:
                c.w("gpsimd", "sT", xbf_guard[slot])
            xsem = f"sXB{slot}"
            c.dma("gpsimd", ab(xo, ET * CH),
                  xr[:, :, ch * CH:(ch + 1) * CH], inc=xsem)
            c.wdma("tensor", xsem)
            if ch == 0:
                c.wdma("tensor", "sWD")
            lastv = None
            for et in range(ET):
                for kind, woff, bcol in (("a", M_WD, CF_BD), ("v", M_WV, None)):
                    bidx = (ch * 16 + et * 2 + (0 if kind == "a" else 1)) % 4
                    if bfree[bidx]:
                        c.w("tensor", bfree[bidx][0], bfree[bidx][1])
                    mmv = None
                    for ct in range(ET):
                        mmv = c.mm(pbank(bidx),
                                   ab(woff + (et * ET + ct) * 128, 128),
                                   ab(xo + ct * CH, CH),
                                   start=(ct == 0), stop=(ct == ET - 1),
                                   inc="sT" if ct == ET - 1 else None)
                    d = et % 2
                    if kind == "a":
                        c.w("scalar", "sT", mmv)
                        if st.get(f"pafree{d}"):
                            c.w("scalar", "sV", st[f"pafree{d}"])
                        if ch == 0 and et == 0:
                            c.wdma("scalar", "sXD")
                        sv = c.act(af(PA[d], CH), pbank(bidx), AF.Sigmoid,
                                   bias=cf[:, CF_BD + et:CF_BD + et + 1],
                                   inc="sS")
                        bfree[bidx] = ("sS", sv)
                        c.w("vector", "sS", sv)
                        c.v("tensor_scalar", af(PT1[d], CH), af(PA[d], CH),
                            1.0, -1.0, ALU.subtract, ALU.mult, inc="sV")
                    else:
                        c.w("vector", "sT", mmv)
                        if ch == 0 and et == 0:
                            c.wdma("vector", "sXD")
                        vv = c.v("scalar_tensor_tensor", af(PT1[d], CH),
                                 pbank(bidx), cf[:, CF_BV + et:CF_BV + et + 1],
                                 af(PT1[d], CH), ALU.add, ALU.mult, inc="sV")
                        bfree[bidx] = ("sV", vv)
                        c.v("tensor_tensor_scan", stc(et, 1 + ch * CH, CH),
                            af(PA[d], CH), af(PT1[d], CH),
                            stc(et, ch * CH, 1), ALU.mult, ALU.add)
                        if st.get(f"afree{d}"):
                            c.w("vector", "sS", st[f"afree{d}"])
                        ini = 1.0 if ch == 0 else af(ASTATE + et, 1)
                        c.v("tensor_tensor_scan", af(PAF[d], CH),
                            af(PA[d], CH), af(ZRO, CH), ini,
                            ALU.mult, ALU.add)
                        c.v("tensor_copy", af(ASTATE + et, 1),
                            af(PAF[d] + CH - 1, 1), inc="sV")
                        scv = c.val("sV")
                        st["scv"] = scv
                        st[f"pafree{d}"] = scv
                        c.w("scalar", "sV", scv)
                        st[f"afree{d}"] = c.scopy(
                            ab(A_SC + ch * 4096 + et * CH, CH),
                            af(PAF[d], CH), inc="sS")
                        lastv = mmv
            xbf_guard[slot] = lastv
        c.w("vector", "sV", st["scv"])
        hs = None
        for et in range(ET):
            hs = c.v("tensor_copy", af(HSND + et, 1), stc(et, T, 1))
        c.w("gpsimd", "sV", hs)
        c.dma("gpsimd", io["cc_h_in"][:], af(HSND, ET), inc="sCB")
        c.wdma("gpsimd", "sCB")
        st["ccH"] = c.cc("AllGather", [io["cc_h_in"][:]], [io["cc_h_out"][:]],
                         [[0, 1], [2, 3], [4, 5], [6, 7]])

    def ln_pre(c, zbf_off, z2_off, src_ap_fn, chunk=CH, src_batch_ap=None,
               src_wait="all"):
        if src_wait == "prev":
            if st.get("lnfree"):
                c.w("scalar", st["lnfree"][0], st["lnfree"][1])
        else:
            c.w("scalar", "sV", c.val("sV"))
        if src_batch_ap is not None:
            if isinstance(src_batch_ap, list):
                for dst, sap in src_batch_ap:
                    c.scopy(dst, sap)
            else:
                c.scopy(ab(zbf_off, ET * chunk)
                        .rearrange("p (e t) -> p e t", e=ET), src_batch_ap)
        else:
            for et in range(ET):
                c.scopy(ab(zbf_off + et * chunk, chunk), src_ap_fn(et))
        st["zq"] = c.act(ab(z2_off, ET * chunk), ab(zbf_off, ET * chunk),
                         AF.Square, inc="sS")
        return st["zq"]

    def ln_chunk(c, ch, zbf_off, z2_off, skb_off, src_ap_fn, dst_stream,
                 lg=None, lb=None, dst_fn=None, chunk=CH, src_batch_ap=None,
                 pre_done=False, src_wait="all"):
        if not pre_done:
            ln_pre(c, zbf_off, z2_off, src_ap_fn, chunk=chunk,
                   src_batch_ap=src_batch_ap, src_wait=src_wait)
        zq = st["zq"]
        if st.get("lnfree"):
            c.w("tensor", st["lnfree"][0], st["lnfree"][1])
        c.w("tensor", "sS", zq)
        if st.get("cb_wait_t") is None:
            c.w("tensor", "sWD", st["cb"])
            st["cb_wait_t"] = 1
        ntt = chunk // 128
        smv = None
        nsum = 2 * ntt * ET
        k = 0
        for g, soff in ((0, zbf_off), (ntt, z2_off)):
            for tt in range(ntt):
                for et in range(ET):
                    k += 1
                    smv = c.mm(PS[2][:, g + tt:g + tt + 1],
                               ab(soff + et * chunk + tt * 128, 128),
                               cb[:, CB_ONE:CB_ONE + 1],
                               start=(k == 1), stop=(k == nsum),
                               inc="sT" if k == nsum else None)
        c.w("vector", "sT", smv)
        c.v("tensor_scalar_mul", af(SK, ntt), PS[2][:, 0:ntt], 1.0 / D)
        c.v("tensor_scalar_mul", af(SK + 8, ntt), PS[2][:, ntt:2 * ntt],
            1.0 / D)
        c.v("scalar_tensor_tensor", af(SK + 16, ntt),
            af(SK, ntt), -1.0, af(SK, ntt), ALU.mult, ALU.mult)
        c.v("tensor_add", af(SK + 16, ntt), af(SK + 16, ntt), af(SK + 8, ntt))
        c.v("tensor_scalar_add", af(SK + 16, ntt), af(SK + 16, ntt), 1e-5,
            inc="sV")
        c.w("scalar", "sV", c.val("sV"))
        sq = c.act(af(SK + 8, ntt), af(SK + 16, ntt), AF.Sqrt, inc="sS")
        c.w("vector", "sS", sq)
        c.v("reciprocal", af(SK + 16, ntt), af(SK + 8, ntt))
        c.v("scalar_tensor_tensor", af(SK + 24, ntt),
            af(SK, ntt), -1.0, af(SK + 16, ntt),
            ALU.mult, ALU.mult)
        c.v("tensor_copy", ab(skb_off + 2048, ntt), af(SK + 16, ntt))
        c.v("tensor_copy", ab(skb_off + 2056, ntt), af(SK + 24, ntt),
            inc="sV")
        c.w("vector", "sV", c.val("sV"))
        c.v("tensor_sub", ab(skb_off + 2064, ntt), af(SK + 16, ntt),
            ab(skb_off + 2048, ntt))
        hl0 = c.v("tensor_sub", ab(skb_off + 2072, ntt), af(SK + 24, ntt),
                  ab(skb_off + 2056, ntt), inc="sV")
        c.w("tensor", "sV", hl0)
        tv2 = None
        nt2 = 4 * ntt
        k2 = 0
        for scol in (skb_off + 2048, skb_off + 2064,
                     skb_off + 2056, skb_off + 2072):
            for tt in range(ntt):
                c0 = (k2 // ntt) * 512 + tt * 128
                k2 += 1
                tv2 = c.trans(PT[0:1, c0:c0 + 128], ab(scol + tt, 1),
                              cb[:, CB_ID:CB_ID + 128],
                              inc="sT" if k2 == nt2 else None)
        c.w("scalar", "sT", tv2)
        er = c.act(ab(skb_off, 4 * 512, rows=1), PT[0:1, 0:2048], AF.Copy,
                   inc="sS")
        c.w("tensor", "sS", er)
        c.mm(PS[2][:, 0:chunk], cb[0:1, CB_ONR:CB_ONR + 128],
             ab(skb_off, chunk, rows=1), start=True, stop=False)
        c.mm(PS[2][:, 0:chunk], cb[0:1, CB_ONR:CB_ONR + 128],
             ab(skb_off + 512, chunk, rows=1), start=False, stop=True)
        c.mm(PS[2][:, 512:512 + chunk], cb[0:1, CB_ONR:CB_ONR + 128],
             ab(skb_off + 1024, chunk, rows=1), start=True, stop=False)
        bcv = c.mm(PS[2][:, 512:512 + chunk], cb[0:1, CB_ONR:CB_ONR + 128],
                   ab(skb_off + 1536, chunk, rows=1), start=False, stop=True,
                   inc="sT")
        c.w("vector", "sT", bcv)
        if dst_stream:
            sview = (stream.ap()
                     .rearrange("p (e s) -> p e s", e=ET)
                     [:, :, 1 + ch * chunk:1 + (ch + 1) * chunk])
            rbc = bass.AP(PS[2], 0, [[1024, 128], [0, ET], [1, chunk]])
            mbc = bass.AP(PS[2], 512, [[1024, 128], [0, ET], [1, chunk]])
            c.v("tensor_mul", sview, sview, rbc)
            lnv = c.v("tensor_add", sview, sview, mbc)
        else:
            if st.get("lnt_free"):
                c.w("vector", st["lnt_free"][0], st["lnt_free"][1])
            zv3 = ab(zbf_off, ET * chunk).rearrange("p (e t) -> p e t", e=ET)
            t3 = af(0, ET * chunk).rearrange("p (e t) -> p e t", e=ET)
            rbc = bass.AP(PS[2], 0, [[1024, 128], [0, ET], [1, chunk]])
            mbc = bass.AP(PS[2], 512, [[1024, 128], [0, ET], [1, chunk]])
            c.v("tensor_mul", t3, zv3, rbc)
            tvv = c.v("tensor_add", t3, t3, mbc, inc="sV")
            c.w("scalar", "sV", tvv)
            lnv = None
            for et in range(ET):
                lnv = c.act(dst_fn(et), af(et * chunk, chunk), AF.Identity,
                            bias=cf[:, lb + et:lb + et + 1],
                            scale=cf[:, lg + et:lg + et + 1])
            st["lnt_free"] = ("sS", lnv)
            st["lnready"] = ("sS", lnv)
            st["lnfree"] = ("sV", tvv)
            return lnv
        st["lnfree"] = ("sV", lnv)
        return lnv

    def p2(c):
        c.w("gpsimd", "sT", c.val("sT"))
        st["wqkv_0"] = c.dma("gpsimd", ab(WQKV, 12288), io["wqkv0"][:])
        c.w("gpsimd", "sCC", st["ccH"])
        c.dma("gpsimd", recvh[:], io["cc_h_out"].ap()[0:128, :], inc="sCB")
        c.wdma("vector", "sCB")
        c.wdma("vector", "sXD")
        c.v("tensor_scalar_mul", hincol[:], recvh[:],
            cf[:, CF_FLAG:CF_FLAG + 1], inc="sV")
        st["hin"] = c.val("sV")
        bfree = [None] * 4
        for ch in range(min(NCH, p2_chunks)):
            for et in range(ET):
                d = et % 2
                if st.get(f"zadd{d}"):
                    c.w("sync", "sV", st[f"zadd{d}"])
                if st.get(f"cast{d}"):
                    c.w("sync", "sS", st[f"cast{d}"])
                c.wdma("sync", f"sXA{d}")
                c.dma("sync", af(PA[d], CH),
                      xr[:, et, ch * CH:(ch + 1) * CH], inc=f"sXA{d}")
                c.wdma("scalar", f"sXA{d}")
                st[f"cast{d}"] = c.scopy(
                    ab(M_XBF + et * CH, CH), af(PA[d], CH), inc="sS")
            xbv = c.val("sS")
            c.w("tensor", "sS", xbv)
            if ch == 0:
                c.wdma("tensor", "sWD")
            for et in range(ET):
                d = et % 2
                bidx = (ch * 8 + et) % 4
                if bfree[bidx]:
                    c.w("tensor", bfree[bidx][0], bfree[bidx][1])
                if st.get(f"gfree{d}"):
                    c.w("scalar", "sV", st[f"gfree{d}"])
                gm = None
                for ct in range(ET):
                    gm = c.mm(pbank(bidx), ab(M_WG + (et * ET + ct) * 128, 128),
                              ab(M_XBF + ct * CH, CH),
                              start=(ct == 0), stop=(ct == ET - 1),
                              inc="sT" if ct == ET - 1 else None)
                c.w("scalar", "sT", gm)
                gv = c.act(af(PT1[d], CH), pbank(bidx), AF.Sigmoid,
                           bias=cf[:, CF_BG + et:CF_BG + et + 1], inc="sS")
                bfree[bidx] = ("sS", gv)
                c.w("vector", "sS", gv)
                c.w("vector", "sV", st["hin"])
                c.v("scalar_tensor_tensor", af(PAF[d], CH),
                    ab(A_SC + ch * 4096 + et * CH, CH),
                    hincol[:, et:et + 1], stc(et, 1 + ch * CH, CH),
                    ALU.mult, ALU.add)
                st[f"gfree{d}"] = c.v("tensor_mul", af(PAF[d], CH),
                                      af(PAF[d], CH), af(PT1[d], CH))
                c.w("sync", "sS", st[f"cast{d}"])
                if st.get(f"zadd{d}"):
                    c.w("sync", "sV", st[f"zadd{d}"])
                c.wdma("sync", f"sXA{d}")
                c.dma("sync", af(PA[d], CH),
                      xr[:, et, ch * CH:(ch + 1) * CH], inc=f"sXA{d}")
                c.wdma("vector", f"sXA{d}")
                zv = c.v("tensor_add", stc(et, 1 + ch * CH, CH),
                         af(PA[d], CH), af(PAF[d], CH), inc="sV")
                st[f"zadd{d}"] = zv
            if ch < p2_ln:
                ln_chunk(c, ch, M_ZBF, M_Z2, A_SC + ch * 4096,
                         lambda et, ch=ch: stc(et, 1 + ch * CH, CH), True,
                         src_batch_ap=stream.ap()
                         .rearrange("p (e s) -> p e s", e=ET)
                         [:, :, 1 + ch * CH:1 + (ch + 1) * CH])
        send_bq(c, 0)
        if not p2_spill:
            return
        c.w("sync", "sV", c.val("sV"))
        if "x0" in tap_t:
            c.dma("sync", tap_t["x0"].ap().rearrange("p (e t) -> p e t", e=ET),
                  stream.ap().rearrange("p (e s) -> p e s", e=ET)[:, :, 1:SC])
        c.wdma("vector", "sXD")

    GROUPS = [[0, 1], [2, 3], [4, 5], [6, 7]]
    WORDER = [1, 2, 3, 0]
    KVN = 4 * W + 4 * W + ET

    def tap_stream(c, name):
        if name in tap_t:
            c.w("sync", "sV", c.val("sV"))
            c.dma("sync",
                  tap_t[name].ap().rearrange("p (e t) -> p e t", e=ET),
                  stream.ap().rearrange("p (e s) -> p e s", e=ET)[:, :, 1:SC])
            c.wdma("vector", "sXD")

    def send_bq(c, l):
        bv = None
        for et in range(ET):
            bv = c.v("tensor_copy", bnd[:, et:et + 1], stc(et, T, 1))
        c.w("gpsimd", "sV", bv)
        c.dma("gpsimd", io[f"cc_bq{l}_in"][:], bnd[:, 0:ET], inc="sCB")
        c.wdma("gpsimd", "sCB")
        st[f"ccBQ{l}"] = c.cc("AllGather", [io[f"cc_bq{l}_in"][:]],
                              [io[f"cc_bq{l}_out"][:]], GROUPS)

    def p_qkv(c, l):
        c.w("gpsimd", "sCC", st[f"ccBQ{l}"])
        c.dma("gpsimd", bndr[:, 0:ET], io[f"cc_bq{l}_out"].ap()[0:128, :],
              inc="sCB")
        if l == 0:
            c.dma("gpsimd",
                  io["x1spill"].ap().rearrange("p (e t) -> p e t", e=ET),
                  stream.ap().rearrange("p (e s) -> p e s", e=ET)[:, :, 1:SC],
                  inc="sXB1")
        wq_base, wk_base, wv_base = WQKV, WQKV + 4096, WQKV + 8192
        if l == 1:
            c.w("gpsimd", "sT", c.val("sT"))
            c.dma("gpsimd", ab(wq_base, 4096), io["wqkv1"].ap()[:, 0:4096])
            c.dma("gpsimd", ab(wk_base, 4096), io["wqkv1"].ap()[:, 4096:8192],
                  inc="sXB0")
            c.dma("gpsimd", ab(wv_base, 4096),
                  io["wqkv1"].ap()[:, 8192:12288], inc="sXB1")
        c.wdma("sync", "sXD")
        c.dma("sync", af(0, 4096, rows=64), io["tabs"].ap()[:, 0:4096])
        c.wdma("vector", "sXD")
        qsc = cf[0:64, CF_QS + 4 * l + 0:CF_QS + 4 * l + 1]
        ksc = cf[0:64, CF_QS + 4 * l + 2:CF_QS + 4 * l + 3]
        for i in range(8):
            c.v("tensor_scalar_mul",
                ab(TABQ + i * 512, 512, r0=64, rows=64),
                af(i * 512, 512, rows=64), qsc)
        st["qfold"] = c.val("sV")
        for w in WORDER:
            wt = w * W
            ks_slot = (0 if w % 2 == 0 else 1)
            tks = TABK + ks_slot * 1024
            if st.get(f"tkfree{ks_slot}"):
                c.w("sync", "sV", st[f"tkfree{ks_slot}"])
            c.w("sync", "sV", st["qfold"])
            c.wdma("sync", "sXD")
            c.dma("sync",
                  af(ks_slot * 1024, 1024, rows=64)
                  .rearrange("p (k t) -> p k t", k=2),
                  io["tabs"].ap().rearrange("r (k t) -> r k t", k=4)
                  [:, 2:4, wt:wt + W])
            c.wdma("vector", "sXD")
            c.v("tensor_scalar_mul", ab(tks, 512, rows=64),
                af(ks_slot * 1024, 512, rows=64), ksc)
            c.v("tensor_scalar_mul", ab(tks + 512, 512, rows=64),
                af(ks_slot * 1024 + 512, 512, rows=64), ksc)
            if st.get("xbfw_free"):
                c.w("scalar", "sT", st["xbfw_free"])
            for et in range(ET):
                c.scopy(ab(XBFW + et * 513, 513), stc(et, w * W, 513))
            bdv = None
            if w == 0:
                c.wdma("vector", "sCB")
                c.w("vector", "sS", c.val("sS"))
                for et in range(ET):
                    bdv = c.v("tensor_scalar_mul", ab(XBFW + et * 513, 1),
                              bndr[:, et:et + 1], cf[:, CF_FLAG:CF_FLAG + 1])
            c.w("tensor", "sS", c.val("sS"))
            if bdv is not None:
                c.w("tensor", "sV", bdv)
            if w == WORDER[0]:
                c.wdma("tensor", "sWD")
            if st.get("lnfree"):
                c.w("tensor", st["lnfree"][0], st["lnfree"][1])

            def rhs(ct, f0=0, fn=W):
                off = 1 if ct < 4 else 0
                return ab(XBFW + ct * 513 + off + f0, fn)
            sv2 = {}
            def stat_mms(part, scol):
                c.w("tensor", "sS", sv2[part])
                last = None
                for m in range(4):
                    last = c.mm(PS[2][0:8, scol:scol + 512],
                                cb[:, CB_IND + 8 * m:CB_IND + 8 * m + 8],
                                ab(Q2 + m * 512, 512), start=(m == 0),
                                stop=(m == 3), inc="sT" if m == 3 else None)
                st[f"{part}stat"] = last
                return last
            for part, rawoff in (("q", QRAW), ("k", KRAW)):
                wbase = wq_base if part == "q" else wk_base
                if part == "k" and w == WORDER[0] and l == 1:
                    c.wdma("tensor", "sXB0")
                for m in range(4):
                    bidx = m % 2
                    if st.get(f"bk{bidx}"):
                        c.w("tensor", st[f"bk{bidx}"][0], st[f"bk{bidx}"][1])
                    mv = None
                    for ct in range(ET):
                        mv = c.mm(pbank(bidx),
                                  ab(wbase + (m * ET + ct) * 128, 128),
                                  rhs(ct), start=(ct == 0),
                                  stop=(ct == ET - 1),
                                  inc="sT" if ct == ET - 1 else None)
                    c.w("scalar", "sT", mv)
                    if st.get(f"raw{part}{m}"):
                        c.w("scalar", "sV", st[f"raw{part}{m}"])
                    ev = c.scopy(ab(rawoff + m * 512, 512), pbank(bidx))
                    st[f"bk{bidx}"] = ("sS", ev)
                if part == "q":
                    if st.get("kstat"):
                        c.w("scalar", "sT", st["kstat"])
                    sv2["q"] = c.act(ab(Q2, 2048), ab(QRAW, 2048), AF.Square,
                                     inc="sS")
            def norm_chain(part, scol):
                c.w("vector", "sT", st[f"{part}stat"])
                c.v("tensor_scalar_max", af(2048 + scol, 512, rows=8),
                    PS[2][0:8, scol:scol + 512], 1e-24)
                c.w("scalar", "sV", c.val("sV"))
                sq = c.act(af(2048 + scol, 512, rows=8),
                           af(2048 + scol, 512, rows=8), AF.Sqrt)
                c.w("vector", "sS", sq)
                c.v("reciprocal", af(2048 + scol, 512, rows=8),
                    af(2048 + scol, 512, rows=8))
                return c.v("tensor_copy", ab(NQB + scol, 512, rows=8),
                           af(2048 + scol, 512, rows=8))
            stat_mms("q", 0)
            c.w("scalar", "sT", st["qstat"])
            sv2["k"] = c.act(ab(Q2, 2048), ab(KRAW, 2048), AF.Square,
                             inc="sS")
            nq_rdy = norm_chain("q", 0)
            if w == WORDER[0] and l == 1:
                c.wdma("tensor", "sXB1")
            for sub in range(4):
                bidx = 2 + sub % 2
                if st.get(f"bk{bidx}"):
                    c.w("tensor", st[f"bk{bidx}"][0], st[f"bk{bidx}"][1])
                mv = None
                for ct in range(ET):
                    off = 1 if ct < 4 else 0
                    mv = c.mm(pbank(bidx),
                              ab(XBFW + ct * 513 + off + sub * 128, 128),
                              ab(wv_base + ct * 512, 512),
                              start=(ct == 0), stop=(ct == ET - 1),
                              inc="sT" if ct == ET - 1 else None)
                c.w("scalar", "sT", mv)
                st[f"bk{bidx}"] = ("sS", c.scopy(
                    ab(VT + (w * 4 + sub) * 512, 512), pbank(bidx)))
            st["xbfw_free"] = c.val("sT")
            stat_mms("k", 512)
            nk_rdy = norm_chain("k", 512)
            st["lnfree"] = ("sV", nk_rdy)
            for h in range(HEADS):
                hb = 64 * (h % 2)
                tile = h // 2
                bidx = h % 4
                if st.get(f"bk{bidx}"):
                    c.w("tensor", st[f"bk{bidx}"][0], st[f"bk{bidx}"][1])
                c.w("tensor", "sV", nq_rdy)
                bv_ = c.mm(pbank(bidx, rows=64),
                           cb[0:8, CB_SEL + 64 * h:CB_SEL + 64 * (h + 1)],
                           ab(NQB, 512, rows=8),
                           start=True, stop=True, inc="sT")
                c.w("vector", "sT", bv_)
                uv = c.v("tensor_mul",
                         ab(KU + h * 2048 + wt, 512, r0=64, rows=64),
                         ab(QRAW + tile * 512, 512, r0=hb, rows=64),
                         pbank(bidx, rows=64))
                st[f"bk{bidx}"] = ("sV", uv)
                st[f"rawq{tile}"] = uv
            kpv = None
            for h in range(HEADS):
                hb = 64 * (h % 2)
                tile = h // 2
                bidx = h % 4
                if st.get(f"bk{bidx}"):
                    c.w("tensor", st[f"bk{bidx}"][0], st[f"bk{bidx}"][1])
                c.w("tensor", "sV", nk_rdy)
                bv_ = c.mm(pbank(bidx, rows=64),
                           cb[0:8, CB_SEL + 64 * h:CB_SEL + 64 * (h + 1)],
                           ab(NQB + 512, 512, rows=8),
                           start=True, stop=True, inc="sT")
                c.w("vector", "sT", bv_)
                kh = c.v("tensor_mul", ab(KHAT, 512, rows=64),
                         ab(KRAW + tile * 512, 512, r0=hb, rows=64),
                         pbank(bidx, rows=64))
                st[f"bk{bidx}"] = ("sV", kh)
                st[f"rawk{tile}"] = kh
                kc = KU + h * 2048 + wt
                c.v("tensor_mul", ab(kc, 512, rows=64),
                    ab(KHAT, 512, rows=64), ab(tks, 512, rows=64))
                c.v("tensor_mul", ab(KTMP, 512, rows=32),
                    ab(KHAT, 512, r0=32, rows=32),
                    ab(tks + 512, 512, r0=32, rows=32))
                c.v("tensor_mul", ab(KTMP, 512, r0=32, rows=32),
                    ab(KHAT, 512, rows=32),
                    ab(tks + 512, 512, rows=32))
                c.v("tensor_add", ab(kc, 512, rows=32),
                    ab(kc, 512, rows=32), ab(KTMP, 512, rows=32))
                kpv = c.v("tensor_add", ab(kc, 512, r0=32, rows=32),
                          ab(kc, 512, r0=32, rows=32),
                          ab(KTMP, 512, r0=32, rows=32))
            st[f"tkfree{ks_slot}"] = c.val("sV")
            if w == 3:
                bd_ = None
                for et in range(ET):
                    bd_ = c.v("tensor_copy", bnd[:, et:et + 1], stc(et, T, 1))
                c.w("gpsimd", "sV", c.val("sV"))
                kvin = io[f"cc_kv{l}_in"]
                c.dma("gpsimd",
                      kvin.ap()[0:64, 0:4 * W]
                      .rearrange("p (h t) -> p h t", h=4),
                      AB.ap()[0:64, KU:KU + 4 * 2048]
                      .rearrange("p (h t) -> p h t", h=4)[:, :, wt:wt + W],
                      inc="sCB")
                c.dma("gpsimd",
                      kvin.ap()[64:128, 0:4 * W]
                      .rearrange("p (h t) -> p h t", h=4),
                      AB.ap()[0:64, KU + 4 * 2048:KU + 8 * 2048]
                      .rearrange("p (h t) -> p h t", h=4)[:, :, wt:wt + W],
                      inc="sCB")
                c.w("gpsimd", "sS", c.val("sS"))
                c.dma("gpsimd", kvin.ap()[:, 4 * W:8 * W],
                      ab(VT + 12 * 512, 2048), inc="sCB")
                c.dma("gpsimd", kvin.ap()[:, 8 * W:KVN],
                      bnd[:, 0:ET], inc="sCB")
                c.wdma("gpsimd", "sCB")
                st["ccKV"] = c.cc("AllGather", [kvin[:]],
                                  [io[f"cc_kv{l}_out"][:]], GROUPS)

    def p_attn(c, l):
        c.w("gpsimd", "sT", c.val("sT"))
        c.w("gpsimd", "sCC", st["ccKV"])
        c.wdma("vector", "sXB1")
        c.dma("gpsimd", ab(WO, 4096), io[f"wo{l}"][:])
        c.dma("gpsimd", ab(RCV, 4 * W, rows=64),
              io[f"cc_kv{l}_out"].ap()[0:64, 0:4 * W], inc="sCB")
        c.dma("gpsimd", ab(RCV + 4 * W, 4 * W, rows=64),
              io[f"cc_kv{l}_out"].ap()[64:128, 0:4 * W], inc="sCB")
        st["kv_rcv"] = c.dma("gpsimd", ab(RCV + ET * W, 4 * W),
                             io[f"cc_kv{l}_out"].ap()[0:128, 4 * W:8 * W],
                             inc="sCB")

        def qbuild(w, h):
            wt = w * W
            qslot = QTB + ((w * 8 + h) % 2) * 1024
            if st.get(f"qtb{(w * 8 + h) % 2}"):
                c.w("vector", "sT", st[f"qtb{(w * 8 + h) % 2}"])
            ucols = KU + h * 2048 + wt
            c.v("tensor_mul", ab(qslot, 512, rows=64),
                ab(ucols, 512, r0=64, rows=64),
                ab(TABQ + wt, 512, r0=64, rows=64))
            c.v("tensor_mul", ab(qslot + 512, 512, rows=32),
                ab(ucols, 512, r0=96, rows=32),
                ab(TABQ + 2048 + wt, 512, r0=96, rows=32))
            return c.v("tensor_mul", ab(qslot + 512, 512, r0=32, rows=32),
                       ab(ucols, 512, r0=64, rows=32),
                       ab(TABQ + 2048 + wt, 512, r0=64, rows=32))

        PRBS = (PRB, PRB + 640, PRB3)
        RS = (SK + 2, SK + 3, SK + 7)

        def transposes(w, h, ic, fi, nrmv):
            ATTB = ATT if (w * 8 + h) % 2 == 0 else ATT2
            prb = PRBS[fi % 3]
            ts = (fi % 2) * 1024
            if st.get(f"ptev{fi % 2}"):
                c.w("tensor", "sS", st[f"ptev{fi % 2}"])
            c.w("tensor", "sV", nrmv)
            tv = None
            for j in range(5):
                tv = c.trans(PT[:, ts + j * 128:ts + j * 128 + 128],
                             ab(prb + j * 128, 128), cb[:, CB_ID:CB_ID + 128],
                             inc="sT" if j == 4 else None)
            c.w("scalar", "sT", tv)
            if st.get(f"attf{(w * 8 + h) % 2}"):
                c.w("scalar", "sT", st[f"attf{(w * 8 + h) % 2}"])
            n_own = ic + 1
            ev = c.act(
                bass.AP(AB, ATTB + ic * 128,
                        [[NBF, 128], [512, n_own], [1, 128]]),
                bass.AP(PT, ts, [[2048, 128], [128, n_own], [1, 128]]),
                AF.Copy)
            n_prev = 4 - ic
            if n_prev:
                ev = c.act(
                    bass.AP(AB, ATTB + (4 + ic) * 512 + ic * 128,
                            [[NBF, 128], [512, n_prev], [1, 128]]),
                    bass.AP(PT, ts + n_own * 128,
                            [[2048, 128], [128, n_prev], [1, 128]]),
                    AF.Copy)
            st[f"prbf{fi % 3}"] = c.val("sT")
            st[f"ptev{fi % 2}"] = ev
            st[f"evT{(w * 8 + h) % 2}"] = ev
            return ev

        def av_stage(w, h):
            par = (w * 8 + h) % 2
            ATTB = ATT if par == 0 else ATT2
            c.w("tensor", "sS", st[f"evT{par}"])
            if st.get("avf"):
                c.w("tensor", st["avf"][0], st["avf"][1])
            first = True
            avv = None
            for jt in range(4):
                i0 = jt * 128
                fn = 512 - i0
                avv = c.mm(PS[2][0:64, i0:512],
                           ab(VT + (w * 4 + jt) * 512 + h * 64, 64),
                           ab(ATTB + jt * 512 + i0, fn),
                           start=first, stop=False)
                first = False
            for jt in range(4):
                fn = (jt + 1) * 128
                last = jt == 3
                if w == 0:
                    lhs = ab(RCV + ET * W + jt * 512 + h * 64, 64)
                else:
                    lhs = ab(VT + ((w - 1) * 4 + jt) * 512 + h * 64, 64)
                avv = c.mm(PS[2][0:64, 0:fn], lhs,
                           ab(ATTB + (4 + jt) * 512, fn),
                           start=False, stop=last,
                           inc="sT" if last else None)
            st[f"attf{par}"] = avv
            c.w("scalar", "sT", avv)
            st["avf"] = ("sS", c.scopy(
                ab(OTW + (h // 2) * 512, 512, r0=64 * (h % 2), rows=64),
                PS[2][0:64, 0:512]))

        def normstage(fi, ex):
            prbn = PRBS[fi % 3]
            c.w("vector", "sS", ex)
            c.v("reciprocal", af(SK + 4, 1), af(RS[fi % 3], 1))
            return c.v("tensor_scalar_mul", ab(prbn, 640), ab(prbn, 640),
                       af(SK + 4, 1))

        pend = []
        fctr = [0]

        def stage_tick():
            if pend and pend[-1].get("nrm") is None:
                e = pend[-1]
                e["nrm"] = normstage(e["fi"], e["ex"])
            if len(pend) >= 2:
                e = pend.pop(0)
                transposes(e["w"], e["h"], e["ic"], e["fi"], e["nrm"])

        def stage_flush():
            while pend:
                if pend[0].get("nrm") is None:
                    pend[0]["nrm"] = normstage(pend[0]["fi"], pend[0]["ex"])
                e = pend.pop(0)
                transposes(e["w"], e["h"], e["ic"], e["fi"], e["nrm"])

        for wi, w in enumerate(WORDER):
            wt = w * W
            for h in range(HEADS):
                qslot = QTB + ((w * 8 + h) % 2) * 1024
                if h == 0:
                    qbv = qbuild(w, h)
                    st["qbv"] = qbv
                qbv = st["qbv"]
                nxt = qbuild(w, h + 1) if h + 1 < HEADS else None
                if nxt:
                    st["qbv"] = nxt
                for ic in range(4):
                    stage_tick()
                    fi = fctr[0]
                    jo = (ic + 1) * 128
                    pl = (4 - ic) * 128
                    sim = PS[ic % 2]
                    if st.get(f"sim{ic % 2}"):
                        c.w("tensor", "sS", st[f"sim{ic % 2}"])
                    for bb in ((0, 1) if ic % 2 == 0 else (2, 3)):
                        if st.get(f"bk{bb}"):
                            c.w("tensor", st[f"bk{bb}"][0], st[f"bk{bb}"][1])
                    c.w("tensor", "sV", qbv)
                    if w == 0:
                        c.wdma("tensor", "sCB")
                    c.mm(sim[:, 0:jo], ab(qslot + ic * 128, 128, rows=64),
                         ab(KU + h * 2048 + wt, jo, rows=64),
                         start=True, stop=False)
                    c.mm(sim[:, 0:jo], ab(qslot + 512 + ic * 128, 128, rows=64),
                         ab(KU + h * 2048 + wt, jo, rows=64),
                         start=False, stop=False)
                    if w == 0:
                        prevk = lambda j0, fn: ab(
                            RCV + h * 512 + ic * 128 + j0, fn, rows=64)
                    else:
                        prevk = lambda j0, fn: ab(
                            KU + h * 2048 + (w - 1) * 512 + ic * 128 + j0,
                            fn, rows=64)
                    segs = [(jo, 512 - jo), (512, 128)] if jo < 512 else [(512, 128)]
                    off = 0
                    for (c0, fn) in segs:
                        c.mm(sim[:, c0:c0 + fn],
                             ab(qslot + ic * 128, 128, rows=64),
                             prevk(off, fn), start=(c0 == 512), stop=False)
                        c.mm(sim[:, c0:c0 + fn],
                             ab(qslot + 512 + ic * 128, 128, rows=64),
                             prevk(off, fn),
                             start=False,
                             stop=(c0 == 512 and w != 0 and ic < 3))
                        off += fn
                    segsm = ([(jo - 128, 256)] if jo <= 384 else
                             [(jo - 128, 128), (jo, 128)])
                    for si, (m0, mn) in enumerate(segsm):
                        if m0 >= 512:
                            mstop = (w != 0)
                        else:
                            mstop = not (w == 0 and ic < 3)
                        last_all = (w != 0 and si == len(segsm) - 1)
                        c.mm(sim[:, m0:m0 + mn], cb[:, CB_ID:CB_ID + 128],
                             cb[:, CB_M2 + (m0 - (jo - 128)):
                                 CB_M2 + (m0 - (jo - 128)) + mn],
                             start=False, stop=mstop,
                             inc="sT" if last_all else None)
                    if w == 0:
                        segs0 = ([(jo, 512 - jo), (512, 128)] if jo < 512
                                 else [(512, 128)])
                        o2 = 0
                        for si, (m0, mn) in enumerate(segs0):
                            last = si == len(segs0) - 1
                            c.mm(sim[:, m0:m0 + mn], cb[:, CB_ID:CB_ID + 128],
                                 cb[:, CB_MW0 + o2:CB_MW0 + o2 + mn],
                                 start=False, stop=True,
                                 inc="sT" if last else None)
                            o2 += mn
                    simv = c.val("sT")
                    st[f"qtb{(w * 8 + h) % 2}"] = simv
                    c.w("vector", "sT", simv)
                    c.v("reduce_max", af(SK + 5 + (fi % 2), 1), sim[:, 0:640],
                        axis=AX.X, negate=True)
                    c.w("scalar", "sV", c.val("sV"))
                    prb = PRBS[fi % 3]
                    if st.get(f"prbf{fi % 3}"):
                        c.w("scalar", "sT", st[f"prbf{fi % 3}"])
                    ex = c.act(ab(prb, 640), sim[:, 0:640], AF.Exp,
                               bias=af(SK + 5 + (fi % 2), 1),
                               accum=af(RS[fi % 3], 1))
                    st[f"sim{ic % 2}"] = ex
                    pend.append({"w": w, "h": h, "ic": ic, "fi": fi, "ex": ex})
                    fctr[0] += 1
                if h >= 1:
                    av_stage(w, h - 1)
            stage_flush()
            av_stage(w, HEADS - 1)
            c.w("tensor", "sS", c.val("sS"))
            if st.get("lnfree"):
                c.w("tensor", st["lnfree"][0], st["lnfree"][1])
            c.wdma("tensor", "sWD")
            for m in range(ET):
                bidx = m % 4
                if st.get(f"bk{bidx}"):
                    c.w("tensor", st[f"bk{bidx}"][0], st[f"bk{bidx}"][1])
                mv = None
                for ct in range(4):
                    mv = c.mm(pbank(bidx), ab(WO + (m * 4 + ct) * 128, 128),
                              ab(OTW + ct * 512, 512),
                              start=(ct == 0), stop=(ct == 3),
                              inc="sT" if ct == 3 else None)
                c.w("vector", "sT", mv)
                st[f"bk{bidx}"] = ("sV", c.v(
                    "tensor_add", stc(m, 1 + wt, W), stc(m, 1 + wt, W),
                    pbank(bidx)))
            if w == 3:
                for et in range(ET):
                    c.v("tensor_copy", bnd[:, et:et + 1], stc(et, T, 1))
                c.w("gpsimd", "sV", c.val("sV"))
                c.dma("gpsimd", io[f"cc_b{l}_in"][:], bnd[:, 0:ET], inc="sCB")
                c.wdma("gpsimd", "sCB")
                st["ccB"] = c.cc("AllGather", [io[f"cc_b{l}_in"][:]],
                                 [io[f"cc_b{l}_out"][:]], GROUPS)

    def p_ffn(c, l):
        c.w("gpsimd", "sCC", st["ccB"])
        st["b_rcv"] = c.dma("gpsimd", bndr[:, 0:ET],
                            io[f"cc_b{l}_out"].ap()[0:128, :], inc="sCB")
        c.wdma("vector", "sCB")
        bval = None
        for et in range(ET):
            bval = c.v("tensor_scalar_mul", stc(et, 0, 1), bndr[:, et:et + 1],
                       cf[:, CF_FLAG:CF_FLAG + 1])
        c.w("gpsimd", "sT", c.val("sT"))
        h0 = QT[0] * 1024
        st["sl0"] = c.dma("gpsimd", ab(SL_A, h0),
                          io[f"w1q{l}_0"].ap()[:, 0:h0], inc="sWA")
        c.dma("gpsimd", ab(SL_A + h0, h0),
              io[f"w1q{l}_0"].ap()[:, h0:2 * h0], inc="sWA")
        st["sl0b"] = c.dma("gpsimd", ab(SL_A + 12288, QT[0] * 1024),
                           io[f"w2q{l}_0"][:], inc="sXB0")
        def shifted(et, ch):
            off = 1 if et < 4 else 0
            return stc(et, off + ch * CH, CH)
        def ffn_sb2(ch):
            return [
                (ab(F_ZBF, 4 * CH).rearrange("p (e t) -> p e t", e=4),
                 stream.ap().rearrange("p (e s) -> p e s", e=ET)
                 [:, 0:4, 1 + ch * CH:1 + (ch + 1) * CH]),
                (ab(F_ZBF + 4 * CH, 4 * CH).rearrange("p (e t) -> p e t", e=4),
                 stream.ap().rearrange("p (e s) -> p e s", e=ET)
                 [:, 4:8, ch * CH:(ch + 1) * CH]),
            ]
        def ffn_ln(c, ch, pre_done=False):
            ln_chunk(c, ch, F_ZBF, F_Z2, F_SKB,
                     lambda et, ch=ch: shifted(et, ch), False,
                     lg=CF_LNG + 16 * l, lb=CF_LNG + 16 * l + 8,
                     dst_fn=lambda et, ch=ch: ab(LNH + et * 2048 + ch * CH, CH),
                     src_batch_ap=ffn_sb2(ch), pre_done=pre_done,
                     src_wait="prev")
        c.w("scalar", "sV", bval)
        ffn_ln(c, 0)
        def prefetch_q(q):
            nslot = SL_A if q % 2 == 0 else SL_B
            if q == 1:
                c.w("gpsimd", "sT", c.val("sT"))
                c.w("gpsimd", "sV", c.val("sV"))
                c.w("gpsimd", "sS", c.val("sS"))
            if st.get(f"slfree{q % 2}"):
                c.w("gpsimd", "sT", st[f"slfree{q % 2}"])
            wsem = "sWA" if q % 2 == 0 else "sWB"
            hw = QT[q] * 1024
            c.dma("gpsimd", ab(nslot, hw), io[f"w1q{l}_{q}"].ap()[:, 0:hw],
                  inc=wsem)
            c.dma("gpsimd", ab(nslot + hw, hw),
                  io[f"w1q{l}_{q}"].ap()[:, hw:2 * hw], inc=wsem)
            c.dma("gpsimd", ab(nslot + 12288, QT[q] * 1024),
                  io[f"w2q{l}_{q}"][:],
                  inc="sXB0" if q % 2 == 0 else "sXB1")

        for q in range(4):
            slot = SL_A if q % 2 == 0 else SL_B
            c.wdma("tensor", "sWA" if q % 2 == 0 else "sWB")
            if st.get("lnready"):
                c.w("tensor", st["lnready"][0], st["lnready"][1])
            if l == 1 and q == 1:
                c.w("sync", st["lnt_free"][0], st["lnt_free"][1])
                fin_reload(c, 0)
                st["fin_pf"] = 1
            if q >= 1 and q + 1 < 4:
                prefetch_q(q + 1)
            for ch in range(NCH):
                if q == 0:
                    if ch + 1 < NCH:
                        ln_pre(c, F_ZBF, F_Z2, None,
                               src_batch_ap=ffn_sb2(ch + 1), src_wait="prev")
                    if st.get("lnready"):
                        c.w("tensor", st["lnready"][0], st["lnready"][1])
                hvs = []
                for mi in range(QT[q]):
                    gi = QOF[q] + mi
                    mvs = {}
                    for ag in range(2):
                        bidx = (mi * 2 + ag) % 4
                        if st.get(f"bk{bidx}"):
                            c.w("tensor", st[f"bk{bidx}"][0],
                                st[f"bk{bidx}"][1])
                        mv = None
                        for ct in range(ET):
                            mv = c.mm(pbank(bidx),
                                      ab(slot + (mi * 2 + ag) * 1024 + ct * 128, 128),
                                      ab(LNH + ct * 2048 + ch * CH, CH),
                                      start=(ct == 0), stop=(ct == ET - 1),
                                      inc="sT" if ct == ET - 1 else None)
                        mvs[ag] = (bidx, mv)
                    gb, gv = mvs[1]
                    c.w("scalar", "sT", gv)
                    if st.get("gelf"):
                        c.w("scalar", "sV", st["gelf"])
                    ge = c.act(ab(GEL, 512), pbank(gb), AF.Gelu,
                               bias=cf[:, CF_B1A + 44 * l + 22 + gi:
                                       CF_B1A + 44 * l + 23 + gi])
                    st[f"bk{gb}"] = ("sS", ge)
                    ab_, av_ = mvs[0]
                    c.w("vector", "sT", av_)
                    c.w("vector", "sS", ge)
                    hv = c.v("scalar_tensor_tensor", ab(HBU + mi * 512, 512),
                             pbank(ab_),
                             cf[:, CF_B1A + 44 * l + gi:CF_B1A + 44 * l + 1 + gi],
                             ab(GEL, 512), ALU.add, ALU.mult)
                    st["gelf"] = hv
                    st[f"bk{ab_}"] = ("sV", hv)
                    hvs.append(hv)
                if q == 0 and ch + 1 < NCH:
                    ffn_ln(c, ch + 1, pre_done=True)
                    if ch + 1 == NCH - 1:
                        prefetch_q(1)
                if ch == 0:
                    c.wdma("tensor", "sXB0" if q % 2 == 0 else "sXB1")
                for mo in range(ET):
                    bidx = mo % 4
                    if st.get(f"bk{bidx}"):
                        c.w("tensor", st[f"bk{bidx}"][0], st[f"bk{bidx}"][1])
                    mv = None
                    for ci in range(QT[q]):
                        c.w("tensor", "sV", hvs[ci])
                        mv = c.mm(pbank(bidx),
                                  ab(slot + 12288 + (mo * QT[q] + ci) * 128, 128),
                                  ab(HBU + ci * 512, 512),
                                  start=(ci == 0), stop=(ci == QT[q] - 1),
                                  inc="sT" if ci == QT[q] - 1 else None)
                    c.w("vector", "sT", mv)
                    if q < 3:
                        rv = c.v("tensor_add", stc(mo, 1 + ch * CH, CH),
                                 stc(mo, 1 + ch * CH, CH), pbank(bidx))
                    else:
                        rv = c.v("scalar_tensor_tensor",
                                 stc(mo, 1 + ch * CH, CH), pbank(bidx),
                                 cf[:, CF_B2 + 8 * l + mo:CF_B2 + 8 * l + mo + 1],
                                 stc(mo, 1 + ch * CH, CH), ALU.add, ALU.add)
                    st[f"bk{bidx}"] = ("sV", rv)
            st[f"slfree{q % 2}"] = c.val("sT")
        if l == 0:
            send_bq(c, 1)

    def fin_reload(c, ch):
        c.dma("sync", af(0, 2048).rearrange("p (e t) -> p e t", e=4),
              io["x1spill"].ap().rearrange("p (e t) -> p e t", e=ET)
              [:, 0:4, ch * CH:(ch + 1) * CH], inc="sXA0")
        c.dma("sync", af(2048, 2048).rearrange("p (e t) -> p e t", e=4),
              io["x1spill"].ap().rearrange("p (e t) -> p e t", e=ET)
              [:, 4:8, ch * CH:(ch + 1) * CH], inc="sXA0")

    def p_fin(c):
        if not st.get("fin_pf"):
            if st.get("lnt_free"):
                c.w("sync", st["lnt_free"][0], st["lnt_free"][1])
            fin_reload(c, 0)
        for ch in range(NCH):
            c.wdma("vector", "sXA0")
            sview = (stream.ap().rearrange("p (e s) -> p e s", e=ET)
                     [:, :, 1 + ch * CH:1 + (ch + 1) * CH])
            fz = c.v(
                "tensor_add", sview, sview,
                af(0, ET * CH).rearrange("p (e t) -> p e t", e=ET))
            st["fzadd"] = fz
            if ch + 1 < NCH:
                c.w("sync", "sV", fz)
                fin_reload(c, ch + 1)
            ln_chunk(c, ch, FIN_ZBF, FIN_Z2, FIN_SKB,
                     lambda et, ch=ch: stc(et, 1 + ch * CH, CH), True,
                     src_batch_ap=sview)
            c.w("gpsimd", "sV", c.val("sV"))
            c.dma("gpsimd",
                  io["outT"].ap().rearrange("p (e t) -> p e t", e=ET)
                  [:, :, ch * CH:(ch + 1) * CH],
                  stream.ap().rearrange("p (e s) -> p e s", e=ET)
                  [:, :, 1 + ch * CH:1 + (ch + 1) * CH], inc="sWD")
        c.wdma("sync", "sXD")
        c.wdma("sync", "sXA0")
        c.wdma("gpsimd", "sWD")
        c.wdma("gpsimd", "sCB")

    def run():
        def fin0(c):
            c.w("sync", "sXD", c.val("sXD"))
            c.w("gpsimd", "sWD", c.val("sWD"))
        K.block("p0", p0)
        if stop_after == "p0":
            K.block("fin", fin0)
            return
        K.block("p1", p1)
        if stop_after == "p1":
            K.block("fin", fin0)
            return
        K.block("p2", p2)
        if stop_after == "mixer":
            def fin(c):
                c.w("sync", "sXD", c.val("sXD"))
                c.w("gpsimd", "sWD", c.val("sWD"))
            K.block("fin", fin)
            return
        for l in range(2):
            K.block(f"qkv{l}", lambda c, l=l: p_qkv(c, l))
            K.block(f"attn{l}", lambda c, l=l: p_attn(c, l))
            if l == 0 and "x05_0" in tap_t:
                K.block("tap05", lambda c: tap_stream(c, "x05_0"))
            K.block(f"ffn{l}", lambda c, l=l: p_ffn(c, l))
            if l == 0 and "x1_0" in tap_t:
                K.block("tap1", lambda c: tap_stream(c, "x1_0"))
            if stop_after == f"l{l}":
                def fin(c):
                    c.w("sync", "sXD", c.val("sXD"))
                    c.w("gpsimd", "sWD", c.val("sWD"))
                    c.w("gpsimd", "sCB", c.val("sCB"))
                K.block("fin", fin)
                return
        K.block("fin", p_fin)

    run()
    es.close()
    return nc, io, tap_t



def _tile_image(Wm, ctn, mtn):
    a = Wm.reshape(ctn, 128, mtn, 128).transpose(1, 2, 0, 3)
    return np.ascontiguousarray(a.reshape(128, mtn * ctn * 128))


def _etile_image(Xm):
    Dd, Tt = Xm.shape
    a = Xm.reshape(Dd // 128, 128, Tt).transpose(1, 0, 2)
    return np.ascontiguousarray(a.reshape(128, (Dd // 128) * Tt))


def _percol(v):
    return np.ascontiguousarray(np.asarray(v, np.float32).reshape(-1, 128).T)


def make_tables(pos):
    half = np.arange(0, DH, 2, dtype=np.float64)
    inv_freq = 1.0 / 10000.0 ** (half / DH)
    t = pos.astype(np.float64)
    freqs = np.concatenate([t[:, None] * inv_freq[None, :]] * 2, axis=1)
    base = (half + 0.4 * DH) / (1.4 * DH)
    power = (t - S // 2) / float(W // 2)
    scale = base[None, :] ** power[:, None]
    scale = np.concatenate([scale, scale], axis=1)
    cos, sin = np.cos(freqs), np.sin(freqs)
    Aq = 8.0 * (cos * scale).T
    Bq = 8.0 * (sin * scale).T
    Ak = (cos / scale).T
    Bk = (sin / scale).T
    Bq[:32] *= -1.0
    Bk[:32] *= -1.0
    BqX = np.concatenate([Bq[32:], Bq[:32]], axis=0)
    BkX = np.concatenate([Bk[32:], Bk[:32]], axis=0)
    return np.ascontiguousarray(
        np.concatenate([Aq, BqX, Ak, BkX], axis=1).astype(np.float32))


def prep_inputs(inputs):
    f32 = np.float32
    x = np.asarray(inputs["L0_attn_x"], f32)
    shared = {}
    shared["wgT"] = _tile_image(np.ascontiguousarray(
        np.asarray(inputs["L0_attn_Wg"], f32).T), ET, ET)
    shared["wvT"] = _tile_image(np.ascontiguousarray(
        np.asarray(inputs["L0_attn_Wv"], f32).T), ET, ET)
    shared["wdT"] = _tile_image(np.ascontiguousarray(
        np.asarray(inputs["L0_attn_Wd"], f32).T), ET, ET)
    wqkv = np.asarray(inputs["L0_ffn_wqkv"], f32)
    wo = np.asarray(inputs["L0_ffn_wo"], f32)
    w1 = np.asarray(inputs["L0_ffn_w1"], f32)
    w2 = np.asarray(inputs["L0_ffn_w2"], f32)
    for l in range(2):
        qk_img = _tile_image(np.ascontiguousarray(wqkv[l][:, :1024]), ET, ET)
        v_img = _etile_image(np.ascontiguousarray(wqkv[l][:, 1024:1536]))
        shared[f"wqkv{l}"] = np.ascontiguousarray(
            np.concatenate([qk_img, v_img], axis=1))
        shared[f"wo{l}"] = _tile_image(wo[l], 4, ET)
        w1a = np.zeros((D, HTILES * 128), f32)
        w1g = np.zeros((D, HTILES * 128), f32)
        w1a[:, :HID] = w1[l][:, :HID]
        w1g[:, :HID] = w1[l][:, HID:]
        w2p = np.zeros((HTILES * 128, D), f32)
        w2p[:HID] = w2[l]
        for q in range(4):
            mt = QT[q]
            sl = slice(QOF[q] * 128, (QOF[q] + mt) * 128)
            ia = w1a[:, sl].reshape(ET, 128, mt, 128).transpose(1, 2, 0, 3)
            ig = w1g[:, sl].reshape(ET, 128, mt, 128).transpose(1, 2, 0, 3)
            img = np.stack([ia, ig], axis=2)
            shared[f"w1q{l}_{q}"] = np.ascontiguousarray(
                img.reshape(128, mt * 2 * ET * 128))
            i2 = w2p[sl, :].reshape(mt, 128, ET, 128).transpose(1, 2, 0, 3)
            shared[f"w2q{l}_{q}"] = np.ascontiguousarray(
                i2.reshape(128, ET * mt * 128))

    def mk_constf(h):
        cfb = np.zeros((128, CF_COLS), f32)
        cfb[:, CF_BG:CF_BG + 8] = _percol(inputs["L0_attn_bg"])
        cfb[:, CF_BV:CF_BV + 8] = _percol(inputs["L0_attn_bv"])
        cfb[:, CF_BD:CF_BD + 8] = _percol(inputs["L0_attn_bd"])
        b2 = np.asarray(inputs["L0_ffn_b2"], f32)
        lng = np.asarray(inputs["L0_ffn_ln_g"], f32)
        lnb = np.asarray(inputs["L0_ffn_ln_b"], f32)
        b1 = np.asarray(inputs["L0_ffn_b1"], f32)
        qs = np.asarray(inputs["L0_ffn_q_scale"], f32)
        ks = np.asarray(inputs["L0_ffn_k_scale"], f32)
        for l in range(2):
            cfb[:, CF_B2 + 8 * l:CF_B2 + 8 * l + 8] = _percol(b2[l])
            cfb[:, CF_LNG + 16 * l:CF_LNG + 16 * l + 8] = _percol(lng[l])
            cfb[:, CF_LNG + 16 * l + 8:CF_LNG + 16 * l + 16] = _percol(lnb[l])
            b1a = np.zeros(HTILES * 128, f32)
            b1g = np.zeros(HTILES * 128, f32)
            b1a[:HID] = b1[l][:HID]
            b1g[:HID] = b1[l][HID:]
            cfb[:, CF_B1A + 44 * l:CF_B1A + 44 * l + 22] = _percol(b1a)
            cfb[:, CF_B1A + 44 * l + 22:CF_B1A + 44 * l + 44] = _percol(b1g)
            cfb[0:64, CF_QS + 4 * l + 0] = qs[l]
            cfb[0:64, CF_QS + 4 * l + 1] = np.concatenate([qs[l][32:], qs[l][:32]])
            cfb[0:64, CF_QS + 4 * l + 2] = ks[l]
            cfb[0:64, CF_QS + 4 * l + 3] = np.concatenate([ks[l][32:], ks[l][:32]])
        cfb[:, CF_FLAG] = 1.0 if h == 1 else 0.0
        ii = np.arange(128)
        cfb[:, CF_MO:CF_MO + 128] = np.where(ii[None, :] <= ii[:, None], 0, NEG)
        cfb[:, CF_MP:CF_MP + 128] = np.where(ii[None, :] >= ii[:, None], 0, NEG)
        cfb[:, CF_MW0:CF_MW0 + 512] = (NEG if h == 0 else 0.0)
        return cfb
    cfs = [mk_constf(0), mk_constf(1)]
    def mk_cb(h):
        cbb = np.zeros((128, CB_COLS), f32)
        cbb[:, CB_ID:CB_ID + 128] = np.eye(128, dtype=f32)
        for m in range(4):
            cbb[0:64, CB_IND + 8 * m + 2 * m] = 1.0
            cbb[64:128, CB_IND + 8 * m + 2 * m + 1] = 1.0
        for hh in range(8):
            cbb[hh, CB_SEL + 64 * hh:CB_SEL + 64 * (hh + 1)] = 1.0
        cbb[:, CB_ONE] = 1.0
        cbb[0, CB_ONR:CB_ONR + 128] = 1.0
        ii = np.arange(128)
        cbb[:, CB_M2:CB_M2 + 128] = np.where(ii[None, :] <= ii[:, None], 0, NEG)
        cbb[:, CB_M2 + 128:CB_M2 + 256] = np.where(
            ii[None, :] >= ii[:, None], 0, NEG)
        cbb[:, CB_MW0:CB_MW0 + 512] = (NEG if h == 0 else 0.0)
        return cbb
    cbs = [mk_cb(0), mk_cb(1)]

    in_maps = []
    for core in range(8):
        b, h = core // 2, core % 2
        m = dict(shared)
        xt = np.ascontiguousarray(x[b, h * T:(h + 1) * T, :].T)
        m["xT"] = _etile_image(xt)
        m["tabs"] = make_tables(np.arange(h * T, (h + 1) * T))
        m["constf"] = cfs[h]
        m["constb"] = cbs[h]
        in_maps.append(m)
    return in_maps


def assemble_out(results, key="outT"):
    out = np.empty((4, S, D), np.float32)
    for core in range(8):
        b, h = core // 2, core % 2
        img = np.asarray(results[core][key])
        xt = img.reshape(128, ET, T).transpose(1, 0, 2).reshape(D, T)
        out[b, h * T:(h + 1) * T, :] = xt.T
    return out


def kernel(**inputs):
    nc, io, taps = build_program()
    in_maps = prep_inputs(inputs)
    res = run_bass_kernel_spmd(nc, in_maps, core_ids=list(range(8)))
    return assemble_out(res.results)


# version 20
# speedup vs baseline: 1.0114x; 1.0114x over previous
import numpy as np
import concourse.bass as bass
import concourse.mybir as mybir
from concourse.bass_utils import run_bass_kernel_spmd

F32 = mybir.dt.float32
BF16 = mybir.dt.bfloat16
AF = mybir.ActivationFunctionType
ALU = mybir.AluOpType
AX = mybir.AxisListType

D = 1024
ET = 8
T = 2048
S = 4096
W = 512
NW = T // W
HEADS = 8
DH = 64
HID = 2730
HTILES = 22
QT = [6, 6, 5, 5]
QOF = [0, 6, 12, 17]
NEG = -1e30
SC = T + 1
MC = 128
NMC = T // MC
ENGS = ("sync", "gpsimd", "tensor", "vector", "scalar")

NBF = 57344
A_SC = 0
M_WD = 16384
M_WV = 24576
M_WG = 32768
M_XBF = 40960
M_ZBF = 49152
M_Z2 = 53248
WQKV = 16384
KU = 0
WQKV = 16384
VT = 28672
TABQ = 36864
XBFW = 40960
QRAW = 45064
KRAW = 47112
TABK = 49160
KHAT = 51208
KTMP = 51720
Q2 = 52232
NQB = 54280
WO = 16384
RCV = 40960
PRB = 47112
PRB3 = 56584
ATT = 48392
ATT2 = 20480
OTW = 52488
QTB = 54536
SL_A = 0
SL_B = 18432
LNH = 36864
HBU = 53248
F_ZBF = 18432
F_Z2 = 22528
F_SKB = 26624
GEL = 56320
FIN_ZBF = 0
FIN_Z2 = 4096
FIN_SKB = 8192

RECV_K = 0
RECV_V = ET * W
RECV_B = ET * W + 4 * W

NF32 = 6144
PA = [0, 512]
PT1 = [1024, 1536]
PAF = [2048, 2560]
ZRO = 3072
HSND = 3584
ASTATE = 3592
SK = 4096
FINX = 0

CF_COLS = 1024
CF_BG = 0
CF_BV = 8
CF_BD = 16
CF_B2 = 24
CF_LNG = 40
CF_B1A = 72
CF_QS = 160
CF_FLAG = 168
CF_MO = 176
CF_MP = 304
CF_MW0 = 432
CB_COLS = 1600
CB_ID = 0
CB_IND = 128
CB_SEL = 160
CB_ONE = 672
CB_ONR = 673
CB_M2 = 802
CB_MW0 = 1058


class Ctx:
    def __init__(self, K, eng, e):
        self.K = K
        self.eng = eng
        self.e = e
        self.ctr = dict(K.base_ctr)
        self.waited = {g: dict(K.base_waited[g]) for g in ENGS}

    def inc(self, s, n=1):
        self.ctr[s] = self.ctr.get(s, 0) + n
        return self.ctr[s]

    def val(self, s):
        return self.ctr.get(s, 0)

    def w(self, eng, s, v):
        if v is None or v <= 0:
            return
        if self.waited[eng].get(s, 0) >= v:
            return
        self.waited[eng][s] = v
        if self.eng == eng:
            self.e.wait_ge(self.K.sems[s], v)

    def _emit(self, eng, inc, n, build):
        if eng in ("vector", "scalar"):
            s = "sV" if eng == "vector" else "sS"
            prev = self.val(s)
            v = self.inc(s)
            need_wait = prev > 0 and self.waited[eng].get(s, 0) < prev
            if self.eng == eng:
                i = build(self.e).then_inc(self.K.sems[s], 1)
                if need_wait:
                    i._wait_ge(self.K.sems[s], prev)
            if prev > 0:
                self.waited[eng][s] = max(self.waited[eng].get(s, 0), prev)
            return v
        v = self.inc(inc, n) if inc else None
        if self.eng == eng:
            i = build(self.e)
            if inc:
                i.then_inc(self.K.sems[inc], n)
        return v

    def mm(self, out, lhsT, rhs, start, stop, inc=None):
        return self._emit("tensor", inc, 1,
                          lambda e: e.matmul(out, lhsT, rhs, start=start, stop=stop))

    def trans(self, out, in_, iden, inc=None):
        return self._emit("tensor", inc, 1,
                          lambda e: e.transpose(out, in_, iden))

    def act(self, out, in_, func, bias=0.0, scale=1.0, accum=None, inc=None):
        return self._emit("scalar", inc, 1,
                          lambda e: e.activation(out, in_, func, bias=bias,
                                                 scale=scale, accum_out=accum))

    def scopy(self, out, in_, inc=None):
        return self.act(out, in_, AF.Copy, inc=inc)

    def v(self, name, *a, inc=None, **kw):
        return self._emit("vector", inc, 1,
                          lambda e: getattr(e, name)(*a, **kw))

    def dma(self, q, dst, src, inc=None, **kw):
        sem = inc or ("sWD" if q == "gpsimd" else "sXD")
        return self._emit(q, sem, 16,
                          lambda e: e.dma_start(out=dst, in_=src, **kw))

    def cc(self, kind, ins, outs, groups, inc="sCC"):
        return self._emit("gpsimd", inc, 1,
                          lambda e: e.collective_compute(
                              kind, ALU.bypass, replica_groups=groups,
                              ins=ins, outs=outs))

    def wdma(self, eng, s):
        self.w(eng, s, self.val(s))

    def nop_inc(self, eng, s, n=1):
        return self._emit(eng, s, n, lambda e: e.engine_nop()
                          if eng in ("vector", "gpsimd") else e.nop())


class Builder:
    def __init__(self, nc, sems, shared):
        self.nc = nc
        self.sems = sems
        self.shared = shared
        self.base_ctr = {}
        self.base_waited = {g: {} for g in ENGS}

    def block(self, name, fn):
        final = {}
        st0 = dict(self.shared)
        with self.nc.Block() as blk:
            for eng in ENGS:
                def section(e, eng=eng):
                    self.shared.clear()
                    self.shared.update(st0)
                    c = Ctx(self, eng, e)
                    fn(c)
                    if "ctr" in final:
                        assert final["ctr"] == c.ctr, (
                            f"replay divergence in block {name}: "
                            f"{final['ctr']} vs {c.ctr}")
                    final["ctr"] = c.ctr
                    final["waited"] = c.waited
                getattr(blk, eng)(section)
        self.base_ctr = final["ctr"]
        self.base_waited = final["waited"]
        for g in ENGS:
            for s in ("sT", "sV", "sS"):
                self.base_waited[g][s] = max(self.base_waited[g].get(s, 0),
                                             self.base_ctr.get(s, 0))


def build_program(taps=(), depth_layers=2, stop_after=None, p2_ln=4, p2_spill=True, p2_chunks=4):
    nc = bass.Bass("TRN2", target_bir_lowering=False)
    io = {}
    io["xT"] = nc.dram_tensor("xT", [128, ET * T], F32, kind="ExternalInput")
    io["outT"] = nc.dram_tensor("outT", [128, ET * T], F32, kind="ExternalOutput")
    for nm in ("wgT", "wvT", "wdT"):
        io[nm] = nc.dram_tensor(nm, [128, ET * 1024], F32, kind="ExternalInput")
    for l in range(2):
        io[f"wqkv{l}"] = nc.dram_tensor(f"wqkv{l}", [128, ET * 1536], F32, kind="ExternalInput")
        io[f"wo{l}"] = nc.dram_tensor(f"wo{l}", [128, 4 * 1024], F32, kind="ExternalInput")
        for q in range(4):
            io[f"w1q{l}_{q}"] = nc.dram_tensor(
                f"w1q{l}_{q}", [128, 2 * QT[q] * ET * 128], F32, kind="ExternalInput")
            io[f"w2q{l}_{q}"] = nc.dram_tensor(
                f"w2q{l}_{q}", [128, ET * QT[q] * 128], F32, kind="ExternalInput")
    io["tabs"] = nc.dram_tensor("tabs", [64, 4 * T], F32, kind="ExternalInput")
    io["constf"] = nc.dram_tensor("constf", [128, CF_COLS], F32, kind="ExternalInput")
    io["constb"] = nc.dram_tensor("constb", [128, CB_COLS], F32, kind="ExternalInput")
    io["x1spill"] = nc.dram_tensor("x1spill", [128, ET * T], F32)
    io["cc_h_in"] = nc.dram_tensor("cc_h_in", [128, ET], F32)
    io["cc_h_out"] = nc.dram_tensor("cc_h_out", [256, ET], F32)
    KVC = 4 * W + 4 * W + ET
    for l in range(2):
        io[f"cc_kv{l}_in"] = nc.dram_tensor(f"cc_kv{l}_in", [128, KVC], BF16)
        io[f"cc_kv{l}_out"] = nc.dram_tensor(f"cc_kv{l}_out", [256, KVC], BF16)
        io[f"cc_b{l}_in"] = nc.dram_tensor(f"cc_b{l}_in", [128, ET], BF16)
        io[f"cc_b{l}_out"] = nc.dram_tensor(f"cc_b{l}_out", [256, ET], BF16)
        io[f"cc_bq{l}_in"] = nc.dram_tensor(f"cc_bq{l}_in", [128, ET], BF16)
        io[f"cc_bq{l}_out"] = nc.dram_tensor(f"cc_bq{l}_out", [256, ET], BF16)
    tap_t = {t: nc.dram_tensor("tap_" + t, [128, ET * T], F32,
                               kind="ExternalOutput") for t in taps}

    from contextlib import ExitStack
    es = ExitStack()
    sbuf = lambda n, s, d: es.enter_context(nc.sbuf_tensor(n, s, d))
    psum = lambda n, s, d: es.enter_context(nc.psum_tensor(n, s, d))

    stream = sbuf("stream", [128, ET * SC], F32)
    AB = sbuf("AB", [128, NBF], BF16)
    AFS = sbuf("AFS", [128, NF32], F32)
    cf = sbuf("cf", [128, CF_COLS], F32)
    cb = sbuf("cb", [128, CB_COLS], BF16)
    recvh = sbuf("recvh", [128, ET], F32)
    hincol = sbuf("hincol", [128, ET], F32)
    bnd = sbuf("bnd", [128, ET], BF16)
    bndr = sbuf("bndr", [128, ET], BF16)
    PS = [psum(f"ps{i}", [128, 1024], F32) for i in range(3)]
    PT = psum("pt", [128, 2048], BF16)

    sems = {}
    for s in ("sWD", "sWA", "sWB", "sXB0", "sXB1", "sXA0", "sXA1", "sCB", "sXD", "sCC", "sT", "sV", "sS"):
        sems[s] = es.enter_context(nc.semaphore(s))
    st = {}
    K = Builder(nc, sems, st)

    def ab(off, cols, r0=0, rows=128):
        return AB[r0:r0 + rows, off:off + cols]

    def af(off, cols, r0=0, rows=128):
        return AFS[r0:r0 + rows, off:off + cols]

    def stc(et, c0, n):
        return stream[:, et * SC + c0: et * SC + c0 + n]

    def pbank(i, rows=128, cols=512, r0=0, c0=0):
        t = PS[i // 2]
        base = (i % 2) * 512
        return t[r0:r0 + rows, base + c0:base + c0 + cols]

    xr = io["xT"].ap().rearrange("p (e t) -> p e t", e=ET)

    def p0(c):
        st["cf"] = c.dma("sync", cf[:], io["constf"][:])
        st["cb"] = c.dma("gpsimd", cb[:], io["constb"][:])
        st["wd"] = c.dma("gpsimd", ab(M_WD, 8192), io["wdT"][:])
        st["wv"] = c.dma("gpsimd", ab(M_WV, 8192), io["wvT"][:])
        st["wg"] = c.dma("gpsimd", ab(M_WG, 8192), io["wgT"][:])
        c.wdma("vector", "sXD")
        for et in range(ET):
            c.v("memset", stc(et, 0, 1), 0.0)
        c.v("memset", af(ZRO, 512), 0.0, inc="sV")
        st["p0v"] = c.val("sV")

    CH = 512
    NCH = T // CH

    def p1(c):
        xbf_guard = [None, None]
        bfree = [None] * 4
        for ch in range(NCH):
            slot = ch % 2
            xo = M_XBF + slot * 4096
            if xbf_guard[slot]:
                c.w("gpsimd", "sT", xbf_guard[slot])
            xsem = f"sXB{slot}"
            c.dma("gpsimd", ab(xo, ET * CH),
                  xr[:, :, ch * CH:(ch + 1) * CH], inc=xsem)
            c.wdma("tensor", xsem)
            if ch == 0:
                c.wdma("tensor", "sWD")
            lastv = None
            for et in range(ET):
                for kind, woff, bcol in (("a", M_WD, CF_BD), ("v", M_WV, None)):
                    bidx = (ch * 16 + et * 2 + (0 if kind == "a" else 1)) % 4
                    if bfree[bidx]:
                        c.w("tensor", bfree[bidx][0], bfree[bidx][1])
                    mmv = None
                    for ct in range(ET):
                        mmv = c.mm(pbank(bidx),
                                   ab(woff + (et * ET + ct) * 128, 128),
                                   ab(xo + ct * CH, CH),
                                   start=(ct == 0), stop=(ct == ET - 1),
                                   inc="sT" if ct == ET - 1 else None)
                    d = et % 2
                    if kind == "a":
                        c.w("scalar", "sT", mmv)
                        if st.get(f"pafree{d}"):
                            c.w("scalar", "sV", st[f"pafree{d}"])
                        if ch == 0 and et == 0:
                            c.wdma("scalar", "sXD")
                        sv = c.act(af(PA[d], CH), pbank(bidx), AF.Sigmoid,
                                   bias=cf[:, CF_BD + et:CF_BD + et + 1],
                                   inc="sS")
                        bfree[bidx] = ("sS", sv)
                        c.w("vector", "sS", sv)
                        c.v("tensor_scalar", af(PT1[d], CH), af(PA[d], CH),
                            1.0, -1.0, ALU.subtract, ALU.mult, inc="sV")
                    else:
                        c.w("vector", "sT", mmv)
                        if ch == 0 and et == 0:
                            c.wdma("vector", "sXD")
                        vv = c.v("scalar_tensor_tensor", af(PT1[d], CH),
                                 pbank(bidx), cf[:, CF_BV + et:CF_BV + et + 1],
                                 af(PT1[d], CH), ALU.add, ALU.mult, inc="sV")
                        bfree[bidx] = ("sV", vv)
                        c.v("tensor_tensor_scan", stc(et, 1 + ch * CH, CH),
                            af(PA[d], CH), af(PT1[d], CH),
                            stc(et, ch * CH, 1), ALU.mult, ALU.add)
                        if st.get(f"afree{d}"):
                            c.w("vector", "sS", st[f"afree{d}"])
                        ini = 1.0 if ch == 0 else af(ASTATE + et, 1)
                        c.v("tensor_tensor_scan", af(PAF[d], CH),
                            af(PA[d], CH), af(ZRO, CH), ini,
                            ALU.mult, ALU.add)
                        c.v("tensor_copy", af(ASTATE + et, 1),
                            af(PAF[d] + CH - 1, 1), inc="sV")
                        scv = c.val("sV")
                        st["scv"] = scv
                        st[f"pafree{d}"] = scv
                        c.w("scalar", "sV", scv)
                        st[f"afree{d}"] = c.scopy(
                            ab(A_SC + ch * 4096 + et * CH, CH),
                            af(PAF[d], CH), inc="sS")
                        lastv = mmv
            xbf_guard[slot] = lastv
        c.w("vector", "sV", st["scv"])
        hs = None
        for et in range(ET):
            hs = c.v("tensor_copy", af(HSND + et, 1), stc(et, T, 1))
        c.w("gpsimd", "sV", hs)
        c.dma("gpsimd", io["cc_h_in"][:], af(HSND, ET), inc="sCB")
        c.wdma("gpsimd", "sCB")
        st["ccH"] = c.cc("AllGather", [io["cc_h_in"][:]], [io["cc_h_out"][:]],
                         [[0, 1], [2, 3], [4, 5], [6, 7]])

    def ln_pre(c, zbf_off, z2_off, src_ap_fn, chunk=CH, src_batch_ap=None,
               src_wait="all"):
        if src_wait == "prev":
            if st.get("lnfree"):
                c.w("scalar", st["lnfree"][0], st["lnfree"][1])
        else:
            c.w("scalar", "sV", c.val("sV"))
        if src_batch_ap is not None:
            if isinstance(src_batch_ap, list):
                for dst, sap in src_batch_ap:
                    c.scopy(dst, sap)
            else:
                c.scopy(ab(zbf_off, ET * chunk)
                        .rearrange("p (e t) -> p e t", e=ET), src_batch_ap)
        else:
            for et in range(ET):
                c.scopy(ab(zbf_off + et * chunk, chunk), src_ap_fn(et))
        st["zq"] = c.act(ab(z2_off, ET * chunk), ab(zbf_off, ET * chunk),
                         AF.Square, inc="sS")
        return st["zq"]

    def ln_chunk(c, ch, zbf_off, z2_off, skb_off, src_ap_fn, dst_stream,
                 lg=None, lb=None, dst_fn=None, chunk=CH, src_batch_ap=None,
                 pre_done=False, src_wait="all"):
        if not pre_done:
            ln_pre(c, zbf_off, z2_off, src_ap_fn, chunk=chunk,
                   src_batch_ap=src_batch_ap, src_wait=src_wait)
        zq = st["zq"]
        if st.get("lnfree"):
            c.w("tensor", st["lnfree"][0], st["lnfree"][1])
        c.w("tensor", "sS", zq)
        if st.get("cb_wait_t") is None:
            c.w("tensor", "sWD", st["cb"])
            st["cb_wait_t"] = 1
        ntt = chunk // 128
        smv = None
        nsum = 2 * ntt * ET
        k = 0
        for g, soff in ((0, zbf_off), (ntt, z2_off)):
            for tt in range(ntt):
                for et in range(ET):
                    k += 1
                    smv = c.mm(PS[2][:, g + tt:g + tt + 1],
                               ab(soff + et * chunk + tt * 128, 128),
                               cb[:, CB_ONE:CB_ONE + 1],
                               start=(k == 1), stop=(k == nsum),
                               inc="sT" if k == nsum else None)
        c.w("vector", "sT", smv)
        c.v("tensor_scalar_mul", af(SK, ntt), PS[2][:, 0:ntt], 1.0 / D)
        c.v("tensor_scalar_mul", af(SK + 8, ntt), PS[2][:, ntt:2 * ntt],
            1.0 / D)
        c.v("scalar_tensor_tensor", af(SK + 16, ntt),
            af(SK, ntt), -1.0, af(SK, ntt), ALU.mult, ALU.mult)
        c.v("tensor_add", af(SK + 16, ntt), af(SK + 16, ntt), af(SK + 8, ntt))
        c.v("tensor_scalar_add", af(SK + 16, ntt), af(SK + 16, ntt), 1e-5,
            inc="sV")
        c.w("scalar", "sV", c.val("sV"))
        sq = c.act(af(SK + 8, ntt), af(SK + 16, ntt), AF.Sqrt, inc="sS")
        c.w("vector", "sS", sq)
        c.v("reciprocal", af(SK + 16, ntt), af(SK + 8, ntt))
        c.v("scalar_tensor_tensor", af(SK + 24, ntt),
            af(SK, ntt), -1.0, af(SK + 16, ntt),
            ALU.mult, ALU.mult)
        c.v("tensor_copy", ab(skb_off + 2048, ntt), af(SK + 16, ntt))
        c.v("tensor_copy", ab(skb_off + 2056, ntt), af(SK + 24, ntt),
            inc="sV")
        c.w("vector", "sV", c.val("sV"))
        c.v("tensor_sub", ab(skb_off + 2064, ntt), af(SK + 16, ntt),
            ab(skb_off + 2048, ntt))
        hl0 = c.v("tensor_sub", ab(skb_off + 2072, ntt), af(SK + 24, ntt),
                  ab(skb_off + 2056, ntt), inc="sV")
        c.w("tensor", "sV", hl0)
        tv2 = None
        nt2 = 4 * ntt
        k2 = 0
        for scol in (skb_off + 2048, skb_off + 2064,
                     skb_off + 2056, skb_off + 2072):
            for tt in range(ntt):
                c0 = (k2 // ntt) * 512 + tt * 128
                k2 += 1
                tv2 = c.trans(PT[0:1, c0:c0 + 128], ab(scol + tt, 1),
                              cb[:, CB_ID:CB_ID + 128],
                              inc="sT" if k2 == nt2 else None)
        c.w("scalar", "sT", tv2)
        er = c.act(ab(skb_off, 4 * 512, rows=1), PT[0:1, 0:2048], AF.Copy,
                   inc="sS")
        c.w("tensor", "sS", er)
        c.mm(PS[2][:, 0:chunk], cb[0:1, CB_ONR:CB_ONR + 128],
             ab(skb_off, chunk, rows=1), start=True, stop=False)
        c.mm(PS[2][:, 0:chunk], cb[0:1, CB_ONR:CB_ONR + 128],
             ab(skb_off + 512, chunk, rows=1), start=False, stop=True)
        c.mm(PS[2][:, 512:512 + chunk], cb[0:1, CB_ONR:CB_ONR + 128],
             ab(skb_off + 1024, chunk, rows=1), start=True, stop=False)
        bcv = c.mm(PS[2][:, 512:512 + chunk], cb[0:1, CB_ONR:CB_ONR + 128],
                   ab(skb_off + 1536, chunk, rows=1), start=False, stop=True,
                   inc="sT")
        c.w("vector", "sT", bcv)
        if dst_stream:
            sview = (stream.ap()
                     .rearrange("p (e s) -> p e s", e=ET)
                     [:, :, 1 + ch * chunk:1 + (ch + 1) * chunk])
            rbc = bass.AP(PS[2], 0, [[1024, 128], [0, ET], [1, chunk]])
            mbc = bass.AP(PS[2], 512, [[1024, 128], [0, ET], [1, chunk]])
            c.v("tensor_mul", sview, sview, rbc)
            lnv = c.v("tensor_add", sview, sview, mbc)
        else:
            if st.get("lnt_free"):
                c.w("vector", st["lnt_free"][0], st["lnt_free"][1])
            zv3 = ab(zbf_off, ET * chunk).rearrange("p (e t) -> p e t", e=ET)
            t3 = af(0, ET * chunk).rearrange("p (e t) -> p e t", e=ET)
            rbc = bass.AP(PS[2], 0, [[1024, 128], [0, ET], [1, chunk]])
            mbc = bass.AP(PS[2], 512, [[1024, 128], [0, ET], [1, chunk]])
            c.v("tensor_mul", t3, zv3, rbc)
            tvv = c.v("tensor_add", t3, t3, mbc, inc="sV")
            c.w("scalar", "sV", tvv)
            lnv = None
            for et in range(ET):
                lnv = c.act(dst_fn(et), af(et * chunk, chunk), AF.Identity,
                            bias=cf[:, lb + et:lb + et + 1],
                            scale=cf[:, lg + et:lg + et + 1])
            st["lnt_free"] = ("sS", lnv)
            st["lnready"] = ("sS", lnv)
            st["lnfree"] = ("sV", tvv)
            return lnv
        st["lnfree"] = ("sV", lnv)
        return lnv

    def p2(c):
        c.w("gpsimd", "sT", c.val("sT"))
        st["wqkv_0"] = c.dma("gpsimd", ab(WQKV, 12288), io["wqkv0"][:])
        c.w("gpsimd", "sCC", st["ccH"])
        c.dma("gpsimd", recvh[:], io["cc_h_out"].ap()[0:128, :], inc="sCB")
        c.wdma("vector", "sCB")
        c.wdma("vector", "sXD")
        c.v("tensor_scalar_mul", hincol[:], recvh[:],
            cf[:, CF_FLAG:CF_FLAG + 1], inc="sV")
        st["hin"] = c.val("sV")
        bfree = [None] * 4
        for ch in range(min(NCH, p2_chunks)):
            for et in range(ET):
                d = et % 2
                if st.get(f"zadd{d}"):
                    c.w("sync", "sV", st[f"zadd{d}"])
                if st.get(f"cast{d}"):
                    c.w("sync", "sS", st[f"cast{d}"])
                c.wdma("sync", f"sXA{d}")
                c.dma("sync", af(PA[d], CH),
                      xr[:, et, ch * CH:(ch + 1) * CH], inc=f"sXA{d}")
                c.wdma("scalar", f"sXA{d}")
                st[f"cast{d}"] = c.scopy(
                    ab(M_XBF + et * CH, CH), af(PA[d], CH), inc="sS")
            xbv = c.val("sS")
            c.w("tensor", "sS", xbv)
            if ch == 0:
                c.wdma("tensor", "sWD")
            for et in range(ET):
                d = et % 2
                bidx = (ch * 8 + et) % 4
                if bfree[bidx]:
                    c.w("tensor", bfree[bidx][0], bfree[bidx][1])
                if st.get(f"gfree{d}"):
                    c.w("scalar", "sV", st[f"gfree{d}"])
                gm = None
                for ct in range(ET):
                    gm = c.mm(pbank(bidx), ab(M_WG + (et * ET + ct) * 128, 128),
                              ab(M_XBF + ct * CH, CH),
                              start=(ct == 0), stop=(ct == ET - 1),
                              inc="sT" if ct == ET - 1 else None)
                c.w("scalar", "sT", gm)
                gv = c.act(af(PT1[d], CH), pbank(bidx), AF.Sigmoid,
                           bias=cf[:, CF_BG + et:CF_BG + et + 1], inc="sS")
                bfree[bidx] = ("sS", gv)
                c.w("vector", "sS", gv)
                c.w("vector", "sV", st["hin"])
                c.v("scalar_tensor_tensor", af(PAF[d], CH),
                    ab(A_SC + ch * 4096 + et * CH, CH),
                    hincol[:, et:et + 1], stc(et, 1 + ch * CH, CH),
                    ALU.mult, ALU.add)
                st[f"gfree{d}"] = c.v("tensor_mul", af(PAF[d], CH),
                                      af(PAF[d], CH), af(PT1[d], CH))
                c.w("sync", "sS", st[f"cast{d}"])
                if st.get(f"zadd{d}"):
                    c.w("sync", "sV", st[f"zadd{d}"])
                c.wdma("sync", f"sXA{d}")
                c.dma("sync", af(PA[d], CH),
                      xr[:, et, ch * CH:(ch + 1) * CH], inc=f"sXA{d}")
                c.wdma("vector", f"sXA{d}")
                zv = c.v("tensor_add", stc(et, 1 + ch * CH, CH),
                         af(PA[d], CH), af(PAF[d], CH), inc="sV")
                st[f"zadd{d}"] = zv
            if ch < p2_ln:
                ln_chunk(c, ch, M_ZBF, M_Z2, A_SC + ch * 4096,
                         lambda et, ch=ch: stc(et, 1 + ch * CH, CH), True,
                         src_batch_ap=stream.ap()
                         .rearrange("p (e s) -> p e s", e=ET)
                         [:, :, 1 + ch * CH:1 + (ch + 1) * CH])
        send_bq(c, 0)
        if not p2_spill:
            return
        c.w("sync", "sV", c.val("sV"))
        if "x0" in tap_t:
            c.dma("sync", tap_t["x0"].ap().rearrange("p (e t) -> p e t", e=ET),
                  stream.ap().rearrange("p (e s) -> p e s", e=ET)[:, :, 1:SC])
        c.wdma("vector", "sXD")

    GROUPS = [[0, 1], [2, 3], [4, 5], [6, 7]]
    WORDER = [1, 2, 3, 0]
    KVN = 4 * W + 4 * W + ET

    def tap_stream(c, name):
        if name in tap_t:
            c.w("sync", "sV", c.val("sV"))
            c.dma("sync",
                  tap_t[name].ap().rearrange("p (e t) -> p e t", e=ET),
                  stream.ap().rearrange("p (e s) -> p e s", e=ET)[:, :, 1:SC])
            c.wdma("vector", "sXD")

    def send_bq(c, l):
        bv = None
        for et in range(ET):
            bv = c.v("tensor_copy", bnd[:, et:et + 1], stc(et, T, 1))
        c.w("gpsimd", "sV", bv)
        c.dma("gpsimd", io[f"cc_bq{l}_in"][:], bnd[:, 0:ET], inc="sCB")
        c.wdma("gpsimd", "sCB")
        st[f"ccBQ{l}"] = c.cc("AllGather", [io[f"cc_bq{l}_in"][:]],
                              [io[f"cc_bq{l}_out"][:]], GROUPS)

    def p_qkv(c, l):
        c.w("gpsimd", "sCC", st[f"ccBQ{l}"])
        c.dma("gpsimd", bndr[:, 0:ET], io[f"cc_bq{l}_out"].ap()[0:128, :],
              inc="sCB")
        if l == 0:
            c.dma("gpsimd",
                  io["x1spill"].ap().rearrange("p (e t) -> p e t", e=ET),
                  stream.ap().rearrange("p (e s) -> p e s", e=ET)[:, :, 1:SC],
                  inc="sXB1")
        wq_base, wk_base, wv_base = WQKV, WQKV + 4096, WQKV + 8192
        if l == 1:
            c.w("gpsimd", "sT", c.val("sT"))
            c.dma("gpsimd", ab(wq_base, 4096), io["wqkv1"].ap()[:, 0:4096])
            c.dma("gpsimd", ab(wk_base, 4096), io["wqkv1"].ap()[:, 4096:8192],
                  inc="sXB0")
            c.dma("gpsimd", ab(wv_base, 4096),
                  io["wqkv1"].ap()[:, 8192:12288], inc="sXB1")
        c.wdma("sync", "sXD")
        c.dma("sync", af(0, 4096, rows=64), io["tabs"].ap()[:, 0:4096])
        c.wdma("vector", "sXD")
        qsc = cf[0:64, CF_QS + 4 * l + 0:CF_QS + 4 * l + 1]
        ksc = cf[0:64, CF_QS + 4 * l + 2:CF_QS + 4 * l + 3]
        for i in range(8):
            c.v("tensor_scalar_mul",
                ab(TABQ + i * 512, 512, r0=64, rows=64),
                af(i * 512, 512, rows=64), qsc)
        st["qfold"] = c.val("sV")
        for w in WORDER:
            wt = w * W
            ks_slot = (0 if w % 2 == 0 else 1)
            tks = TABK + ks_slot * 1024
            if st.get(f"tkfree{ks_slot}"):
                c.w("sync", "sV", st[f"tkfree{ks_slot}"])
            c.w("sync", "sV", st["qfold"])
            c.wdma("sync", "sXD")
            c.dma("sync",
                  af(ks_slot * 1024, 1024, rows=64)
                  .rearrange("p (k t) -> p k t", k=2),
                  io["tabs"].ap().rearrange("r (k t) -> r k t", k=4)
                  [:, 2:4, wt:wt + W])
            c.wdma("vector", "sXD")
            c.v("tensor_scalar_mul", ab(tks, 512, rows=64),
                af(ks_slot * 1024, 512, rows=64), ksc)
            c.v("tensor_scalar_mul", ab(tks + 512, 512, rows=64),
                af(ks_slot * 1024 + 512, 512, rows=64), ksc)
            if st.get("xbfw_free"):
                c.w("scalar", "sT", st["xbfw_free"])
            for et in range(ET):
                c.scopy(ab(XBFW + et * 513, 513), stc(et, w * W, 513))
            bdv = None
            if w == 0:
                c.wdma("vector", "sCB")
                c.w("vector", "sS", c.val("sS"))
                for et in range(ET):
                    bdv = c.v("tensor_scalar_mul", ab(XBFW + et * 513, 1),
                              bndr[:, et:et + 1], cf[:, CF_FLAG:CF_FLAG + 1])
            c.w("tensor", "sS", c.val("sS"))
            if bdv is not None:
                c.w("tensor", "sV", bdv)
            if w == WORDER[0]:
                c.wdma("tensor", "sWD")
            if st.get("lnfree"):
                c.w("tensor", st["lnfree"][0], st["lnfree"][1])

            def rhs(ct, f0=0, fn=W):
                off = 1 if ct < 4 else 0
                return ab(XBFW + ct * 513 + off + f0, fn)
            sv2 = {}
            def stat_mms(part, scol):
                c.w("tensor", "sS", sv2[part])
                last = None
                for m in range(4):
                    last = c.mm(PS[2][0:8, scol:scol + 512],
                                cb[:, CB_IND + 8 * m:CB_IND + 8 * m + 8],
                                ab(Q2 + m * 512, 512), start=(m == 0),
                                stop=(m == 3), inc="sT" if m == 3 else None)
                st[f"{part}stat"] = last
                return last
            for part, rawoff in (("q", QRAW), ("k", KRAW)):
                wbase = wq_base if part == "q" else wk_base
                if part == "k" and w == WORDER[0] and l == 1:
                    c.wdma("tensor", "sXB0")
                for m in range(4):
                    bidx = m % 2
                    if st.get(f"bk{bidx}"):
                        c.w("tensor", st[f"bk{bidx}"][0], st[f"bk{bidx}"][1])
                    mv = None
                    for ct in range(ET):
                        mv = c.mm(pbank(bidx),
                                  ab(wbase + (m * ET + ct) * 128, 128),
                                  rhs(ct), start=(ct == 0),
                                  stop=(ct == ET - 1),
                                  inc="sT" if ct == ET - 1 else None)
                    c.w("scalar", "sT", mv)
                    if st.get(f"raw{part}{m}"):
                        c.w("scalar", "sV", st[f"raw{part}{m}"])
                    ev = c.scopy(ab(rawoff + m * 512, 512), pbank(bidx))
                    st[f"bk{bidx}"] = ("sS", ev)
                if part == "q":
                    if st.get("kstat"):
                        c.w("scalar", "sT", st["kstat"])
                    sv2["q"] = c.act(ab(Q2, 2048), ab(QRAW, 2048), AF.Square,
                                     inc="sS")
            def norm_chain(part, scol):
                c.w("vector", "sT", st[f"{part}stat"])
                c.v("tensor_scalar_max", af(2048 + scol, 512, rows=8),
                    PS[2][0:8, scol:scol + 512], 1e-24)
                c.w("scalar", "sV", c.val("sV"))
                sq = c.act(af(2048 + scol, 512, rows=8),
                           af(2048 + scol, 512, rows=8), AF.Sqrt)
                c.w("vector", "sS", sq)
                c.v("reciprocal", af(2048 + scol, 512, rows=8),
                    af(2048 + scol, 512, rows=8))
                return c.v("tensor_copy", ab(NQB + scol, 512, rows=8),
                           af(2048 + scol, 512, rows=8))
            stat_mms("q", 0)
            c.w("scalar", "sT", st["qstat"])
            sv2["k"] = c.act(ab(Q2, 2048), ab(KRAW, 2048), AF.Square,
                             inc="sS")
            nq_rdy = norm_chain("q", 0)
            if w == WORDER[0] and l == 1:
                c.wdma("tensor", "sXB1")
            for sub in range(4):
                bidx = 2 + sub % 2
                if st.get(f"bk{bidx}"):
                    c.w("tensor", st[f"bk{bidx}"][0], st[f"bk{bidx}"][1])
                mv = None
                for ct in range(ET):
                    off = 1 if ct < 4 else 0
                    mv = c.mm(pbank(bidx),
                              ab(XBFW + ct * 513 + off + sub * 128, 128),
                              ab(wv_base + ct * 512, 512),
                              start=(ct == 0), stop=(ct == ET - 1),
                              inc="sT" if ct == ET - 1 else None)
                c.w("scalar", "sT", mv)
                st[f"bk{bidx}"] = ("sS", c.scopy(
                    ab(VT + (w * 4 + sub) * 512, 512), pbank(bidx)))
            st["xbfw_free"] = c.val("sT")
            stat_mms("k", 512)
            nk_rdy = norm_chain("k", 512)
            st["lnfree"] = ("sV", nk_rdy)
            for h in range(HEADS):
                hb = 64 * (h % 2)
                tile = h // 2
                bidx = h % 4
                if st.get(f"bk{bidx}"):
                    c.w("tensor", st[f"bk{bidx}"][0], st[f"bk{bidx}"][1])
                c.w("tensor", "sV", nq_rdy)
                bv_ = c.mm(pbank(bidx, rows=64),
                           cb[0:8, CB_SEL + 64 * h:CB_SEL + 64 * (h + 1)],
                           ab(NQB, 512, rows=8),
                           start=True, stop=True, inc="sT")
                c.w("vector", "sT", bv_)
                uv = c.v("tensor_mul",
                         ab(KU + h * 2048 + wt, 512, r0=64, rows=64),
                         ab(QRAW + tile * 512, 512, r0=hb, rows=64),
                         pbank(bidx, rows=64))
                st[f"bk{bidx}"] = ("sV", uv)
                st[f"rawq{tile}"] = uv
            kpv = None
            for h in range(HEADS):
                hb = 64 * (h % 2)
                tile = h // 2
                bidx = h % 4
                if st.get(f"bk{bidx}"):
                    c.w("tensor", st[f"bk{bidx}"][0], st[f"bk{bidx}"][1])
                c.w("tensor", "sV", nk_rdy)
                bv_ = c.mm(pbank(bidx, rows=64),
                           cb[0:8, CB_SEL + 64 * h:CB_SEL + 64 * (h + 1)],
                           ab(NQB + 512, 512, rows=8),
                           start=True, stop=True, inc="sT")
                c.w("vector", "sT", bv_)
                kh = c.v("tensor_mul", ab(KHAT, 512, rows=64),
                         ab(KRAW + tile * 512, 512, r0=hb, rows=64),
                         pbank(bidx, rows=64))
                st[f"bk{bidx}"] = ("sV", kh)
                st[f"rawk{tile}"] = kh
                kc = KU + h * 2048 + wt
                c.v("tensor_mul", ab(kc, 512, rows=64),
                    ab(KHAT, 512, rows=64), ab(tks, 512, rows=64))
                c.v("tensor_mul", ab(KTMP, 512, rows=32),
                    ab(KHAT, 512, r0=32, rows=32),
                    ab(tks + 512, 512, r0=32, rows=32))
                c.v("tensor_mul", ab(KTMP, 512, r0=32, rows=32),
                    ab(KHAT, 512, rows=32),
                    ab(tks + 512, 512, rows=32))
                c.v("tensor_add", ab(kc, 512, rows=32),
                    ab(kc, 512, rows=32), ab(KTMP, 512, rows=32))
                kpv = c.v("tensor_add", ab(kc, 512, r0=32, rows=32),
                          ab(kc, 512, r0=32, rows=32),
                          ab(KTMP, 512, r0=32, rows=32))
            st[f"tkfree{ks_slot}"] = c.val("sV")
            if w == 3:
                bd_ = None
                for et in range(ET):
                    bd_ = c.v("tensor_copy", bnd[:, et:et + 1], stc(et, T, 1))
                c.w("gpsimd", "sV", c.val("sV"))
                kvin = io[f"cc_kv{l}_in"]
                c.dma("gpsimd",
                      kvin.ap()[0:64, 0:4 * W]
                      .rearrange("p (h t) -> p h t", h=4),
                      AB.ap()[0:64, KU:KU + 4 * 2048]
                      .rearrange("p (h t) -> p h t", h=4)[:, :, wt:wt + W],
                      inc="sCB")
                c.dma("gpsimd",
                      kvin.ap()[64:128, 0:4 * W]
                      .rearrange("p (h t) -> p h t", h=4),
                      AB.ap()[0:64, KU + 4 * 2048:KU + 8 * 2048]
                      .rearrange("p (h t) -> p h t", h=4)[:, :, wt:wt + W],
                      inc="sCB")
                c.w("gpsimd", "sS", c.val("sS"))
                c.dma("gpsimd", kvin.ap()[:, 4 * W:8 * W],
                      ab(VT + 12 * 512, 2048), inc="sCB")
                c.dma("gpsimd", kvin.ap()[:, 8 * W:KVN],
                      bnd[:, 0:ET], inc="sCB")
                c.wdma("gpsimd", "sCB")
                st["ccKV"] = c.cc("AllGather", [kvin[:]],
                                  [io[f"cc_kv{l}_out"][:]], GROUPS)

    def p_attn(c, l):
        c.w("gpsimd", "sT", c.val("sT"))
        c.w("gpsimd", "sCC", st["ccKV"])
        c.wdma("vector", "sXB1")
        c.dma("gpsimd", ab(WO, 4096), io[f"wo{l}"][:])
        c.dma("gpsimd", ab(RCV, 4 * W, rows=64),
              io[f"cc_kv{l}_out"].ap()[0:64, 0:4 * W], inc="sCB")
        c.dma("gpsimd", ab(RCV + 4 * W, 4 * W, rows=64),
              io[f"cc_kv{l}_out"].ap()[64:128, 0:4 * W], inc="sCB")
        st["kv_rcv"] = c.dma("gpsimd", ab(RCV + ET * W, 4 * W),
                             io[f"cc_kv{l}_out"].ap()[0:128, 4 * W:8 * W],
                             inc="sCB")

        def qbuild(w, h):
            wt = w * W
            qslot = QTB + ((w * 8 + h) % 2) * 1024
            if st.get(f"qtb{(w * 8 + h) % 2}"):
                c.w("vector", "sT", st[f"qtb{(w * 8 + h) % 2}"])
            ucols = KU + h * 2048 + wt
            c.v("tensor_mul", ab(qslot, 512, rows=64),
                ab(ucols, 512, r0=64, rows=64),
                ab(TABQ + wt, 512, r0=64, rows=64))
            c.v("tensor_mul", ab(qslot + 512, 512, rows=32),
                ab(ucols, 512, r0=96, rows=32),
                ab(TABQ + 2048 + wt, 512, r0=96, rows=32))
            return c.v("tensor_mul", ab(qslot + 512, 512, r0=32, rows=32),
                       ab(ucols, 512, r0=64, rows=32),
                       ab(TABQ + 2048 + wt, 512, r0=64, rows=32))

        PRBS = (PRB, PRB + 640, PRB3)
        RS = (SK + 2, SK + 3, SK + 7)

        def transposes(w, h, ic, fi, nrmv):
            ATTB = ATT if (w * 8 + h) % 2 == 0 else ATT2
            prb = PRBS[fi % 3]
            ts = (fi % 2) * 1024
            if st.get(f"ptev{fi % 2}"):
                c.w("tensor", "sS", st[f"ptev{fi % 2}"])
            c.w("tensor", "sV", nrmv)
            tv = None
            for j in range(5):
                tv = c.trans(PT[:, ts + j * 128:ts + j * 128 + 128],
                             ab(prb + j * 128, 128), cb[:, CB_ID:CB_ID + 128],
                             inc="sT" if j == 4 else None)
            c.w("scalar", "sT", tv)
            if st.get(f"attf{(w * 8 + h) % 2}"):
                c.w("scalar", "sT", st[f"attf{(w * 8 + h) % 2}"])
            n_own = ic + 1
            ev = c.act(
                bass.AP(AB, ATTB + ic * 128,
                        [[NBF, 128], [512, n_own], [1, 128]]),
                bass.AP(PT, ts, [[2048, 128], [128, n_own], [1, 128]]),
                AF.Copy)
            n_prev = 4 - ic
            if n_prev:
                ev = c.act(
                    bass.AP(AB, ATTB + (4 + ic) * 512 + ic * 128,
                            [[NBF, 128], [512, n_prev], [1, 128]]),
                    bass.AP(PT, ts + n_own * 128,
                            [[2048, 128], [128, n_prev], [1, 128]]),
                    AF.Copy)
            st[f"prbf{fi % 3}"] = c.val("sT")
            st[f"ptev{fi % 2}"] = ev
            st[f"evT{(w * 8 + h) % 2}"] = ev
            return ev

        def av_stage(w, h):
            par = (w * 8 + h) % 2
            ATTB = ATT if par == 0 else ATT2
            c.w("tensor", "sS", st[f"evT{par}"])
            if st.get("avf"):
                c.w("tensor", st["avf"][0], st["avf"][1])
            first = True
            avv = None
            for jt in range(4):
                i0 = jt * 128
                fn = 512 - i0
                avv = c.mm(PS[2][0:64, i0:512],
                           ab(VT + (w * 4 + jt) * 512 + h * 64, 64),
                           ab(ATTB + jt * 512 + i0, fn),
                           start=first, stop=False)
                first = False
            for jt in range(4):
                fn = (jt + 1) * 128
                last = jt == 3
                if w == 0:
                    lhs = ab(RCV + ET * W + jt * 512 + h * 64, 64)
                else:
                    lhs = ab(VT + ((w - 1) * 4 + jt) * 512 + h * 64, 64)
                avv = c.mm(PS[2][0:64, 0:fn], lhs,
                           ab(ATTB + (4 + jt) * 512, fn),
                           start=False, stop=last,
                           inc="sT" if last else None)
            st[f"attf{par}"] = avv
            c.w("scalar", "sT", avv)
            st["avf"] = ("sS", c.scopy(
                ab(OTW + (h // 2) * 512, 512, r0=64 * (h % 2), rows=64),
                PS[2][0:64, 0:512]))

        def normstage(fi, ex):
            prbn = PRBS[fi % 3]
            c.w("vector", "sS", ex)
            c.v("reciprocal", af(SK + 4, 1), af(RS[fi % 3], 1))
            return c.v("tensor_scalar_mul", ab(prbn, 640), ab(prbn, 640),
                       af(SK + 4, 1))

        pend = []
        fctr = [0]

        def stage_tick():
            if pend and pend[-1].get("nrm") is None:
                e = pend[-1]
                e["nrm"] = normstage(e["fi"], e["ex"])
            if len(pend) >= 2:
                e = pend.pop(0)
                transposes(e["w"], e["h"], e["ic"], e["fi"], e["nrm"])

        def stage_flush():
            while pend:
                if pend[0].get("nrm") is None:
                    pend[0]["nrm"] = normstage(pend[0]["fi"], pend[0]["ex"])
                e = pend.pop(0)
                transposes(e["w"], e["h"], e["ic"], e["fi"], e["nrm"])

        for wi, w in enumerate(WORDER):
            wt = w * W
            for h in range(HEADS):
                qslot = QTB + ((w * 8 + h) % 2) * 1024
                if h == 0:
                    qbv = qbuild(w, h)
                    st["qbv"] = qbv
                qbv = st["qbv"]
                nxt = qbuild(w, h + 1) if h + 1 < HEADS else None
                if nxt:
                    st["qbv"] = nxt
                for ic in range(4):
                    stage_tick()
                    fi = fctr[0]
                    jo = (ic + 1) * 128
                    pl = (4 - ic) * 128
                    sim = PS[ic % 2]
                    if st.get(f"sim{ic % 2}"):
                        c.w("tensor", "sS", st[f"sim{ic % 2}"])
                    for bb in ((0, 1) if ic % 2 == 0 else (2, 3)):
                        if st.get(f"bk{bb}"):
                            c.w("tensor", st[f"bk{bb}"][0], st[f"bk{bb}"][1])
                    c.w("tensor", "sV", qbv)
                    if w == 0:
                        c.wdma("tensor", "sCB")
                    c.mm(sim[:, 0:jo], ab(qslot + ic * 128, 128, rows=64),
                         ab(KU + h * 2048 + wt, jo, rows=64),
                         start=True, stop=False)
                    c.mm(sim[:, 0:jo], ab(qslot + 512 + ic * 128, 128, rows=64),
                         ab(KU + h * 2048 + wt, jo, rows=64),
                         start=False, stop=False)
                    if w == 0:
                        prevk = lambda j0, fn: ab(
                            RCV + h * 512 + ic * 128 + j0, fn, rows=64)
                    else:
                        prevk = lambda j0, fn: ab(
                            KU + h * 2048 + (w - 1) * 512 + ic * 128 + j0,
                            fn, rows=64)
                    segs = [(jo, 512 - jo), (512, 128)] if jo < 512 else [(512, 128)]
                    off = 0
                    for (c0, fn) in segs:
                        c.mm(sim[:, c0:c0 + fn],
                             ab(qslot + ic * 128, 128, rows=64),
                             prevk(off, fn), start=(c0 == 512), stop=False)
                        c.mm(sim[:, c0:c0 + fn],
                             ab(qslot + 512 + ic * 128, 128, rows=64),
                             prevk(off, fn),
                             start=False,
                             stop=(c0 == 512 and w != 0 and ic < 3))
                        off += fn
                    segsm = ([(jo - 128, 256)] if jo <= 384 else
                             [(jo - 128, 128), (jo, 128)])
                    for si, (m0, mn) in enumerate(segsm):
                        if m0 >= 512:
                            mstop = (w != 0)
                        else:
                            mstop = not (w == 0 and ic < 3)
                        last_all = (w != 0 and si == len(segsm) - 1)
                        c.mm(sim[:, m0:m0 + mn], cb[:, CB_ID:CB_ID + 128],
                             cb[:, CB_M2 + (m0 - (jo - 128)):
                                 CB_M2 + (m0 - (jo - 128)) + mn],
                             start=False, stop=mstop,
                             inc="sT" if last_all else None)
                    if w == 0:
                        segs0 = ([(jo, 512 - jo), (512, 128)] if jo < 512
                                 else [(512, 128)])
                        o2 = 0
                        for si, (m0, mn) in enumerate(segs0):
                            last = si == len(segs0) - 1
                            c.mm(sim[:, m0:m0 + mn], cb[:, CB_ID:CB_ID + 128],
                                 cb[:, CB_MW0 + o2:CB_MW0 + o2 + mn],
                                 start=False, stop=True,
                                 inc="sT" if last else None)
                            o2 += mn
                    simv = c.val("sT")
                    st[f"qtb{(w * 8 + h) % 2}"] = simv
                    c.w("vector", "sT", simv)
                    c.v("reduce_max", af(SK + 5 + (fi % 2), 1), sim[:, 0:640],
                        axis=AX.X, negate=True)
                    c.w("scalar", "sV", c.val("sV"))
                    prb = PRBS[fi % 3]
                    if st.get(f"prbf{fi % 3}"):
                        c.w("scalar", "sT", st[f"prbf{fi % 3}"])
                    ex = c.act(ab(prb, 640), sim[:, 0:640], AF.Exp,
                               bias=af(SK + 5 + (fi % 2), 1),
                               accum=af(RS[fi % 3], 1))
                    st[f"sim{ic % 2}"] = ex
                    pend.append({"w": w, "h": h, "ic": ic, "fi": fi, "ex": ex})
                    fctr[0] += 1
                if h >= 1:
                    av_stage(w, h - 1)
            stage_flush()
            av_stage(w, HEADS - 1)
            c.w("tensor", "sS", c.val("sS"))
            if st.get("lnfree"):
                c.w("tensor", st["lnfree"][0], st["lnfree"][1])
            c.wdma("tensor", "sWD")
            for m in range(ET):
                bidx = m % 4
                if st.get(f"bk{bidx}"):
                    c.w("tensor", st[f"bk{bidx}"][0], st[f"bk{bidx}"][1])
                mv = None
                for ct in range(4):
                    mv = c.mm(pbank(bidx), ab(WO + (m * 4 + ct) * 128, 128),
                              ab(OTW + ct * 512, 512),
                              start=(ct == 0), stop=(ct == 3),
                              inc="sT" if ct == 3 else None)
                c.w("vector", "sT", mv)
                st[f"bk{bidx}"] = ("sV", c.v(
                    "tensor_add", stc(m, 1 + wt, W), stc(m, 1 + wt, W),
                    pbank(bidx)))
            if w == 3:
                for et in range(ET):
                    c.v("tensor_copy", bnd[:, et:et + 1], stc(et, T, 1))
                c.w("gpsimd", "sV", c.val("sV"))
                c.dma("gpsimd", io[f"cc_b{l}_in"][:], bnd[:, 0:ET], inc="sCB")
                c.wdma("gpsimd", "sCB")
                st["ccB"] = c.cc("AllGather", [io[f"cc_b{l}_in"][:]],
                                 [io[f"cc_b{l}_out"][:]], GROUPS)

    def p_ffn(c, l):
        c.w("gpsimd", "sCC", st["ccB"])
        st["b_rcv"] = c.dma("gpsimd", bndr[:, 0:ET],
                            io[f"cc_b{l}_out"].ap()[0:128, :], inc="sCB")
        c.wdma("vector", "sCB")
        bval = None
        for et in range(ET):
            bval = c.v("tensor_scalar_mul", stc(et, 0, 1), bndr[:, et:et + 1],
                       cf[:, CF_FLAG:CF_FLAG + 1])
        c.w("gpsimd", "sT", c.val("sT"))
        h0 = QT[0] * 1024
        st["sl0"] = c.dma("gpsimd", ab(SL_A, h0),
                          io[f"w1q{l}_0"].ap()[:, 0:h0], inc="sWA")
        c.dma("gpsimd", ab(SL_A + h0, h0),
              io[f"w1q{l}_0"].ap()[:, h0:2 * h0], inc="sWA")
        st["sl0b"] = c.dma("gpsimd", ab(SL_A + 12288, QT[0] * 1024),
                           io[f"w2q{l}_0"][:], inc="sXB0")
        def shifted(et, ch):
            off = 1 if et < 4 else 0
            return stc(et, off + ch * CH, CH)
        def ffn_sb2(ch):
            return [
                (ab(F_ZBF, 4 * CH).rearrange("p (e t) -> p e t", e=4),
                 stream.ap().rearrange("p (e s) -> p e s", e=ET)
                 [:, 0:4, 1 + ch * CH:1 + (ch + 1) * CH]),
                (ab(F_ZBF + 4 * CH, 4 * CH).rearrange("p (e t) -> p e t", e=4),
                 stream.ap().rearrange("p (e s) -> p e s", e=ET)
                 [:, 4:8, ch * CH:(ch + 1) * CH]),
            ]
        def ffn_ln(c, ch, pre_done=False):
            ln_chunk(c, ch, F_ZBF, F_Z2, F_SKB,
                     lambda et, ch=ch: shifted(et, ch), False,
                     lg=CF_LNG + 16 * l, lb=CF_LNG + 16 * l + 8,
                     dst_fn=lambda et, ch=ch: ab(LNH + et * 2048 + ch * CH, CH),
                     src_batch_ap=ffn_sb2(ch), pre_done=pre_done,
                     src_wait="prev")
        c.w("scalar", "sV", bval)
        ffn_ln(c, 0)
        def prefetch_q(q):
            nslot = SL_A if q % 2 == 0 else SL_B
            if q == 1:
                c.w("gpsimd", "sT", c.val("sT"))
                c.w("gpsimd", "sV", c.val("sV"))
                c.w("gpsimd", "sS", c.val("sS"))
            if st.get(f"slfree{q % 2}"):
                c.w("gpsimd", "sT", st[f"slfree{q % 2}"])
            wsem = "sWA" if q % 2 == 0 else "sWB"
            hw = QT[q] * 1024
            c.dma("gpsimd", ab(nslot, hw), io[f"w1q{l}_{q}"].ap()[:, 0:hw],
                  inc=wsem)
            c.dma("gpsimd", ab(nslot + hw, hw),
                  io[f"w1q{l}_{q}"].ap()[:, hw:2 * hw], inc=wsem)
            c.dma("gpsimd", ab(nslot + 12288, QT[q] * 1024),
                  io[f"w2q{l}_{q}"][:],
                  inc="sXB0" if q % 2 == 0 else "sXB1")

        for q in range(4):
            slot = SL_A if q % 2 == 0 else SL_B
            nbk = 4 if q == 0 else 6
            if q == 1:
                st["bk4"] = st["bk5"] = st["lnfree"]
            c.wdma("tensor", "sWA" if q % 2 == 0 else "sWB")
            if st.get("lnready"):
                c.w("tensor", st["lnready"][0], st["lnready"][1])
            if l == 1 and q == 1:
                c.w("sync", st["lnt_free"][0], st["lnt_free"][1])
                fin_reload(c, 0)
                st["fin_pf"] = 1
            if q >= 1 and q + 1 < 4:
                prefetch_q(q + 1)
            for ch in range(NCH):
                if q == 0:
                    if ch + 1 < NCH:
                        ln_pre(c, F_ZBF, F_Z2, None,
                               src_batch_ap=ffn_sb2(ch + 1), src_wait="prev")
                    if st.get("lnready"):
                        c.w("tensor", st["lnready"][0], st["lnready"][1])
                hvs = []
                for mi in range(QT[q]):
                    gi = QOF[q] + mi
                    mvs = {}
                    for ag in range(2):
                        bidx = (mi * 2 + ag) % nbk
                        if st.get(f"bk{bidx}"):
                            c.w("tensor", st[f"bk{bidx}"][0],
                                st[f"bk{bidx}"][1])
                        mv = None
                        for ct in range(ET):
                            mv = c.mm(pbank(bidx),
                                      ab(slot + (mi * 2 + ag) * 1024 + ct * 128, 128),
                                      ab(LNH + ct * 2048 + ch * CH, CH),
                                      start=(ct == 0), stop=(ct == ET - 1),
                                      inc="sT" if ct == ET - 1 else None)
                        mvs[ag] = (bidx, mv)
                    gb, gv = mvs[1]
                    c.w("scalar", "sT", gv)
                    if st.get("gelf"):
                        c.w("scalar", "sV", st["gelf"])
                    ge = c.act(ab(GEL, 512), pbank(gb), AF.Gelu,
                               bias=cf[:, CF_B1A + 44 * l + 22 + gi:
                                       CF_B1A + 44 * l + 23 + gi])
                    st[f"bk{gb}"] = ("sS", ge)
                    ab_, av_ = mvs[0]
                    c.w("vector", "sT", av_)
                    c.w("vector", "sS", ge)
                    hv = c.v("scalar_tensor_tensor", ab(HBU + mi * 512, 512),
                             pbank(ab_),
                             cf[:, CF_B1A + 44 * l + gi:CF_B1A + 44 * l + 1 + gi],
                             ab(GEL, 512), ALU.add, ALU.mult)
                    st["gelf"] = hv
                    st[f"bk{ab_}"] = ("sV", hv)
                    hvs.append(hv)
                if q == 0 and ch + 1 < NCH:
                    ffn_ln(c, ch + 1, pre_done=True)
                    if ch + 1 == NCH - 1:
                        prefetch_q(1)
                if ch == 0:
                    c.wdma("tensor", "sXB0" if q % 2 == 0 else "sXB1")
                for mo in range(ET):
                    bidx = mo % nbk
                    if st.get(f"bk{bidx}"):
                        c.w("tensor", st[f"bk{bidx}"][0], st[f"bk{bidx}"][1])
                    mv = None
                    for ci in range(QT[q]):
                        c.w("tensor", "sV", hvs[ci])
                        mv = c.mm(pbank(bidx),
                                  ab(slot + 12288 + (mo * QT[q] + ci) * 128, 128),
                                  ab(HBU + ci * 512, 512),
                                  start=(ci == 0), stop=(ci == QT[q] - 1),
                                  inc="sT" if ci == QT[q] - 1 else None)
                    c.w("vector", "sT", mv)
                    if q < 3:
                        rv = c.v("tensor_add", stc(mo, 1 + ch * CH, CH),
                                 stc(mo, 1 + ch * CH, CH), pbank(bidx))
                    else:
                        rv = c.v("scalar_tensor_tensor",
                                 stc(mo, 1 + ch * CH, CH), pbank(bidx),
                                 cf[:, CF_B2 + 8 * l + mo:CF_B2 + 8 * l + mo + 1],
                                 stc(mo, 1 + ch * CH, CH), ALU.add, ALU.add)
                    st[f"bk{bidx}"] = ("sV", rv)
            st[f"slfree{q % 2}"] = c.val("sT")
        if l == 0:
            send_bq(c, 1)

    def fin_reload(c, ch):
        c.dma("sync", af(0, 2048).rearrange("p (e t) -> p e t", e=4),
              io["x1spill"].ap().rearrange("p (e t) -> p e t", e=ET)
              [:, 0:4, ch * CH:(ch + 1) * CH], inc="sXA0")
        c.dma("sync", af(2048, 2048).rearrange("p (e t) -> p e t", e=4),
              io["x1spill"].ap().rearrange("p (e t) -> p e t", e=ET)
              [:, 4:8, ch * CH:(ch + 1) * CH], inc="sXA0")

    def p_fin(c):
        if not st.get("fin_pf"):
            if st.get("lnt_free"):
                c.w("sync", st["lnt_free"][0], st["lnt_free"][1])
            fin_reload(c, 0)
        for ch in range(NCH):
            c.wdma("vector", "sXA0")
            sview = (stream.ap().rearrange("p (e s) -> p e s", e=ET)
                     [:, :, 1 + ch * CH:1 + (ch + 1) * CH])
            fz = c.v(
                "tensor_add", sview, sview,
                af(0, ET * CH).rearrange("p (e t) -> p e t", e=ET))
            st["fzadd"] = fz
            if ch + 1 < NCH:
                c.w("sync", "sV", fz)
                fin_reload(c, ch + 1)
            ln_chunk(c, ch, FIN_ZBF, FIN_Z2, FIN_SKB,
                     lambda et, ch=ch: stc(et, 1 + ch * CH, CH), True,
                     src_batch_ap=sview)
            c.w("gpsimd", "sV", c.val("sV"))
            c.dma("gpsimd",
                  io["outT"].ap().rearrange("p (e t) -> p e t", e=ET)
                  [:, :, ch * CH:(ch + 1) * CH],
                  stream.ap().rearrange("p (e s) -> p e s", e=ET)
                  [:, :, 1 + ch * CH:1 + (ch + 1) * CH], inc="sWD")
        c.wdma("sync", "sXD")
        c.wdma("sync", "sXA0")
        c.wdma("gpsimd", "sWD")
        c.wdma("gpsimd", "sCB")

    def run():
        def fin0(c):
            c.w("sync", "sXD", c.val("sXD"))
            c.w("gpsimd", "sWD", c.val("sWD"))
        K.block("p0", p0)
        if stop_after == "p0":
            K.block("fin", fin0)
            return
        K.block("p1", p1)
        if stop_after == "p1":
            K.block("fin", fin0)
            return
        K.block("p2", p2)
        if stop_after == "mixer":
            def fin(c):
                c.w("sync", "sXD", c.val("sXD"))
                c.w("gpsimd", "sWD", c.val("sWD"))
            K.block("fin", fin)
            return
        for l in range(2):
            K.block(f"qkv{l}", lambda c, l=l: p_qkv(c, l))
            K.block(f"attn{l}", lambda c, l=l: p_attn(c, l))
            if l == 0 and "x05_0" in tap_t:
                K.block("tap05", lambda c: tap_stream(c, "x05_0"))
            K.block(f"ffn{l}", lambda c, l=l: p_ffn(c, l))
            if l == 0 and "x1_0" in tap_t:
                K.block("tap1", lambda c: tap_stream(c, "x1_0"))
            if stop_after == f"l{l}":
                def fin(c):
                    c.w("sync", "sXD", c.val("sXD"))
                    c.w("gpsimd", "sWD", c.val("sWD"))
                    c.w("gpsimd", "sCB", c.val("sCB"))
                K.block("fin", fin)
                return
        K.block("fin", p_fin)

    run()
    es.close()
    return nc, io, tap_t



def _tile_image(Wm, ctn, mtn):
    a = Wm.reshape(ctn, 128, mtn, 128).transpose(1, 2, 0, 3)
    return np.ascontiguousarray(a.reshape(128, mtn * ctn * 128))


def _etile_image(Xm):
    Dd, Tt = Xm.shape
    a = Xm.reshape(Dd // 128, 128, Tt).transpose(1, 0, 2)
    return np.ascontiguousarray(a.reshape(128, (Dd // 128) * Tt))


def _percol(v):
    return np.ascontiguousarray(np.asarray(v, np.float32).reshape(-1, 128).T)


def make_tables(pos):
    half = np.arange(0, DH, 2, dtype=np.float64)
    inv_freq = 1.0 / 10000.0 ** (half / DH)
    t = pos.astype(np.float64)
    freqs = np.concatenate([t[:, None] * inv_freq[None, :]] * 2, axis=1)
    base = (half + 0.4 * DH) / (1.4 * DH)
    power = (t - S // 2) / float(W // 2)
    scale = base[None, :] ** power[:, None]
    scale = np.concatenate([scale, scale], axis=1)
    cos, sin = np.cos(freqs), np.sin(freqs)
    Aq = 8.0 * (cos * scale).T
    Bq = 8.0 * (sin * scale).T
    Ak = (cos / scale).T
    Bk = (sin / scale).T
    Bq[:32] *= -1.0
    Bk[:32] *= -1.0
    BqX = np.concatenate([Bq[32:], Bq[:32]], axis=0)
    BkX = np.concatenate([Bk[32:], Bk[:32]], axis=0)
    return np.ascontiguousarray(
        np.concatenate([Aq, BqX, Ak, BkX], axis=1).astype(np.float32))


def prep_inputs(inputs):
    f32 = np.float32
    x = np.asarray(inputs["L0_attn_x"], f32)
    shared = {}
    shared["wgT"] = _tile_image(np.ascontiguousarray(
        np.asarray(inputs["L0_attn_Wg"], f32).T), ET, ET)
    shared["wvT"] = _tile_image(np.ascontiguousarray(
        np.asarray(inputs["L0_attn_Wv"], f32).T), ET, ET)
    shared["wdT"] = _tile_image(np.ascontiguousarray(
        np.asarray(inputs["L0_attn_Wd"], f32).T), ET, ET)
    wqkv = np.asarray(inputs["L0_ffn_wqkv"], f32)
    wo = np.asarray(inputs["L0_ffn_wo"], f32)
    w1 = np.asarray(inputs["L0_ffn_w1"], f32)
    w2 = np.asarray(inputs["L0_ffn_w2"], f32)
    for l in range(2):
        qk_img = _tile_image(np.ascontiguousarray(wqkv[l][:, :1024]), ET, ET)
        v_img = _etile_image(np.ascontiguousarray(wqkv[l][:, 1024:1536]))
        shared[f"wqkv{l}"] = np.ascontiguousarray(
            np.concatenate([qk_img, v_img], axis=1))
        shared[f"wo{l}"] = _tile_image(wo[l], 4, ET)
        w1a = np.zeros((D, HTILES * 128), f32)
        w1g = np.zeros((D, HTILES * 128), f32)
        w1a[:, :HID] = w1[l][:, :HID]
        w1g[:, :HID] = w1[l][:, HID:]
        w2p = np.zeros((HTILES * 128, D), f32)
        w2p[:HID] = w2[l]
        for q in range(4):
            mt = QT[q]
            sl = slice(QOF[q] * 128, (QOF[q] + mt) * 128)
            ia = w1a[:, sl].reshape(ET, 128, mt, 128).transpose(1, 2, 0, 3)
            ig = w1g[:, sl].reshape(ET, 128, mt, 128).transpose(1, 2, 0, 3)
            img = np.stack([ia, ig], axis=2)
            shared[f"w1q{l}_{q}"] = np.ascontiguousarray(
                img.reshape(128, mt * 2 * ET * 128))
            i2 = w2p[sl, :].reshape(mt, 128, ET, 128).transpose(1, 2, 0, 3)
            shared[f"w2q{l}_{q}"] = np.ascontiguousarray(
                i2.reshape(128, ET * mt * 128))

    def mk_constf(h):
        cfb = np.zeros((128, CF_COLS), f32)
        cfb[:, CF_BG:CF_BG + 8] = _percol(inputs["L0_attn_bg"])
        cfb[:, CF_BV:CF_BV + 8] = _percol(inputs["L0_attn_bv"])
        cfb[:, CF_BD:CF_BD + 8] = _percol(inputs["L0_attn_bd"])
        b2 = np.asarray(inputs["L0_ffn_b2"], f32)
        lng = np.asarray(inputs["L0_ffn_ln_g"], f32)
        lnb = np.asarray(inputs["L0_ffn_ln_b"], f32)
        b1 = np.asarray(inputs["L0_ffn_b1"], f32)
        qs = np.asarray(inputs["L0_ffn_q_scale"], f32)
        ks = np.asarray(inputs["L0_ffn_k_scale"], f32)
        for l in range(2):
            cfb[:, CF_B2 + 8 * l:CF_B2 + 8 * l + 8] = _percol(b2[l])
            cfb[:, CF_LNG + 16 * l:CF_LNG + 16 * l + 8] = _percol(lng[l])
            cfb[:, CF_LNG + 16 * l + 8:CF_LNG + 16 * l + 16] = _percol(lnb[l])
            b1a = np.zeros(HTILES * 128, f32)
            b1g = np.zeros(HTILES * 128, f32)
            b1a[:HID] = b1[l][:HID]
            b1g[:HID] = b1[l][HID:]
            cfb[:, CF_B1A + 44 * l:CF_B1A + 44 * l + 22] = _percol(b1a)
            cfb[:, CF_B1A + 44 * l + 22:CF_B1A + 44 * l + 44] = _percol(b1g)
            cfb[0:64, CF_QS + 4 * l + 0] = qs[l]
            cfb[0:64, CF_QS + 4 * l + 1] = np.concatenate([qs[l][32:], qs[l][:32]])
            cfb[0:64, CF_QS + 4 * l + 2] = ks[l]
            cfb[0:64, CF_QS + 4 * l + 3] = np.concatenate([ks[l][32:], ks[l][:32]])
        cfb[:, CF_FLAG] = 1.0 if h == 1 else 0.0
        ii = np.arange(128)
        cfb[:, CF_MO:CF_MO + 128] = np.where(ii[None, :] <= ii[:, None], 0, NEG)
        cfb[:, CF_MP:CF_MP + 128] = np.where(ii[None, :] >= ii[:, None], 0, NEG)
        cfb[:, CF_MW0:CF_MW0 + 512] = (NEG if h == 0 else 0.0)
        return cfb
    cfs = [mk_constf(0), mk_constf(1)]
    def mk_cb(h):
        cbb = np.zeros((128, CB_COLS), f32)
        cbb[:, CB_ID:CB_ID + 128] = np.eye(128, dtype=f32)
        for m in range(4):
            cbb[0:64, CB_IND + 8 * m + 2 * m] = 1.0
            cbb[64:128, CB_IND + 8 * m + 2 * m + 1] = 1.0
        for hh in range(8):
            cbb[hh, CB_SEL + 64 * hh:CB_SEL + 64 * (hh + 1)] = 1.0
        cbb[:, CB_ONE] = 1.0
        cbb[0, CB_ONR:CB_ONR + 128] = 1.0
        ii = np.arange(128)
        cbb[:, CB_M2:CB_M2 + 128] = np.where(ii[None, :] <= ii[:, None], 0, NEG)
        cbb[:, CB_M2 + 128:CB_M2 + 256] = np.where(
            ii[None, :] >= ii[:, None], 0, NEG)
        cbb[:, CB_MW0:CB_MW0 + 512] = (NEG if h == 0 else 0.0)
        return cbb
    cbs = [mk_cb(0), mk_cb(1)]

    in_maps = []
    for core in range(8):
        b, h = core // 2, core % 2
        m = dict(shared)
        xt = np.ascontiguousarray(x[b, h * T:(h + 1) * T, :].T)
        m["xT"] = _etile_image(xt)
        m["tabs"] = make_tables(np.arange(h * T, (h + 1) * T))
        m["constf"] = cfs[h]
        m["constb"] = cbs[h]
        in_maps.append(m)
    return in_maps


def assemble_out(results, key="outT"):
    out = np.empty((4, S, D), np.float32)
    for core in range(8):
        b, h = core // 2, core % 2
        img = np.asarray(results[core][key])
        xt = img.reshape(128, ET, T).transpose(1, 0, 2).reshape(D, T)
        out[b, h * T:(h + 1) * T, :] = xt.T
    return out


def kernel(**inputs):
    nc, io, taps = build_program()
    in_maps = prep_inputs(inputs)
    res = run_bass_kernel_spmd(nc, in_maps, core_ids=list(range(8)))
    return assemble_out(res.results)
